# version 21
# baseline (speedup 1.0000x reference)
import numpy as np
import ml_dtypes
from contextlib import ExitStack

import concourse.bass as bass
import concourse.bacc as bacc
import concourse.mybir as mybir
from concourse.bass_utils import run_bass_kernel_spmd

F32 = mybir.dt.float32
BF16 = mybir.dt.bfloat16
I32 = mybir.dt.int32
AF = mybir.ActivationFunctionType
OP = mybir.AluOpType

D = 4096
NCH = 32
TSS = 256
PRE = 9216
NSS = PRE // TSS
KC = 1024
NSLOT = PRE // KC
NPASS = 4
OWN = 2048
NOSS = OWN // TSS
EPS = 1e-5
BF = ml_dtypes.bfloat16

ENGS = ("sync", "tensor", "vector", "scalar", "gpsimd")


class Sem:
    def __init__(self, nc, es, name):
        self.h = es.enter_context(nc.semaphore(name))
        self.n = 0

    def inc(self, step=1):
        self.n += step
        return self.n


class Plan:
    def __init__(self):
        self.q = {e: [] for e in ENGS}
        self.last_wait = {e: {} for e in ENGS}
        self.state = {}
        self.ev = {e: [] for e in ENGS}

    def emit(self, eng, fn):
        self.q[eng].append(fn)

    def check(self):
        pos = {e: 0 for e in ENGS}
        cnt = {}
        progress = True
        while progress:
            progress = False
            for e in ENGS:
                while pos[e] < len(self.ev[e]):
                    kind, s, v = self.ev[e][pos[e]]
                    if kind == 'w' and cnt.get(id(s), 0) < v:
                        break
                    if kind == 'i':
                        cnt[id(s)] = cnt.get(id(s), 0) + v
                    pos[e] += 1
                    progress = True
        stuck = {e: (pos[e], len(self.ev[e])) for e in ENGS
                 if pos[e] < len(self.ev[e])}
        if stuck:
            detail = {}
            for e in stuck:
                kind, s, v = self.ev[e][pos[e]]
                detail[e] = (pos[e], kind, getattr(s, 'name', '?'), v,
                             cnt.get(id(s), 0))
            raise RuntimeError(f"plan deadlock: {detail}")

    def wait(self, eng, sem, val=None):
        v = sem.n if val is None else val
        assert v <= sem.n, f"wait {v} > planned {sem.n} ({eng})"
        if v <= 0:
            return
        lw = self.last_wait[eng]
        if lw.get(id(sem), -1) >= v:
            return
        lw[id(sem)] = v
        h = sem.h
        self.ev[eng].append(('w', sem, v))
        self.emit(eng, lambda e, h=h, v=v: e.wait_ge(h, v))

    def waitm(self, eng, m):
        if m is not None:
            self.wait(eng, m[0], m[1])

    def barrier(self, sems):
        for e in ENGS:
            for s in sems:
                self.wait(e, s)

    def play(self, nc):
        with nc.Block() as blk:
            @blk.sync
            def _(eng):
                for f in self.q["sync"]:
                    f(eng)

            @blk.tensor
            def _(eng):
                for f in self.q["tensor"]:
                    f(eng)

            @blk.vector
            def _(eng):
                for f in self.q["vector"]:
                    f(eng)

            @blk.scalar
            def _(eng):
                for f in self.q["scalar"]:
                    f(eng)

            @blk.gpsimd
            def _(eng):
                for f in self.q["gpsimd"]:
                    f(eng)


class NS:
    pass


def build_program(debug_spills=False):
    nc = bacc.Bacc("TRN2", target_bir_lowering=False)
    es = ExitStack()

    def din(name, shape, dt):
        return nc.dram_tensor(name, shape, dt, kind="ExternalInput").ap()

    x_fm = din("x_fm", [128, NSS * NCH * TSS], F32)
    x_own = din("x_own", [128, NOSS * NCH * TSS], F32)
    A_fmd = din("A_fm", [128, NCH * 128], F32)
    Bw_fmd = din("Bw_fm", [128, D], F32)
    route_fmd = din("route_fm", [128, NCH * 8], F32)
    bv_qd = din("bv_q", [4, 1024], F32)
    bv_kd = din("bv_k", [4, 1024], F32)
    bv_vd = din("bv_v", [4, 1024], F32)
    identf_d = din("identf", [128, 128], F32)
    mask_tri_d = din("mask_tri", [128, 16 * 512], BF16)
    mask_ones_d = din("mask_ones", [128, 16 * 512], BF16)
    Eb_d = din("E_b", [8, 128], BF16)
    tbl_d = din("tbl", [1, 64], I32)

    out_fm = nc.dram_tensor("out_fm", [D, OWN], F32, kind="ExternalOutput").ap()
    spk = "ExternalOutput" if debug_spills else "Internal"
    spill_xh = nc.dram_tensor("spill_xh", [D, PRE], BF16, kind=spk).ap()
    spill_x1own = nc.dram_tensor("spill_x1own", [D, OWN], F32, kind=spk).ap()
    spill_q = nc.dram_tensor("spill_q", [D, OWN], BF16, kind=spk).ap()
    spill_x2 = nc.dram_tensor("spill_x2", [D, OWN], F32, kind=spk).ap()
    var2d = nc.dram_tensor("var2d", [1, PRE], F32, kind=spk).ap()
    rstdd_t = nc.dram_tensor("rstdd", [1, PRE], F32, kind="Internal")
    rstdd = rstdd_t.ap()

    def mk_sb(stack):
        def f(name, shape, dt):
            return stack.enter_context(nc.sbuf_tensor(name, shape, dt))
        return f

    def mk_ps(stack):
        def f(name, shape, dt=F32):
            return stack.enter_context(nc.psum_tensor(name, shape, dt))
        return f

    sb = mk_sb(es)

    Ab = sb("Ab", [128, NCH * 128], BF16)
    Bwb = sb("Bwb", [128, D], BF16)
    rob = sb("rob", [128, NCH * 8], BF16)
    Eb = sb("Eb_s", [8, 128], BF16)
    onescol = sb("onescol", [128, 1], BF16)
    onesrow = sb("onesrow", [1, 128], BF16)
    ones1f = sb("ones1f", [1, 128], F32)
    Asum_b = sb("Asum_b", [1, 128], BF16)
    Rsum_b = sb("Rsum_b", [1, 8], BF16)
    BwSum_b = sb("BwSum_b", [128, 1], BF16)
    qkb_f = sb("qkb_f", [128, 32], F32)
    vb_f = sb("vb_f", [128, 32], F32)
    identf = sb("identf_s", [128, 128], F32)
    rstdcols = sb("rstdcols", [128, 72], F32)
    epscol = sb("epscol", [128, 1], F32)
    tbl_s = sb("tbl_s", [1, 64], I32)

    P = Plan()
    sx = Sem(nc, es, "sx")
    sq2 = Sem(nc, es, "sq2")
    sg = Sem(nc, es, "sg")
    pe = Sem(nc, es, "pe")
    act = Sem(nc, es, "act")
    dve = Sem(nc, es, "dve")
    pool = Sem(nc, es, "pool")
    sld = [Sem(nc, es, "sld0"), Sem(nc, es, "sld1")]
    sld2 = [Sem(nc, es, "sld20"), Sem(nc, es, "sld21")]
    skk = [Sem(nc, es, "skk0"), Sem(nc, es, "skk1")]
    svv = [Sem(nc, es, "svv0"), Sem(nc, es, "svv1")]
    sv2 = [Sem(nc, es, "sv20"), Sem(nc, es, "sv21")]
    ssp = [Sem(nc, es, "ssp0"), Sem(nc, es, "ssp1")]
    swr = Sem(nc, es, "swr")
    smk = Sem(nc, es, "smk")
    ALLSEMS = (sx, sq2, sg, pe, act, dve, pool,
               sld[0], sld[1], sld2[0], sld2[1], skk[0], skk[1],
               svv[0], svv[1], sv2[0], sv2[1], ssp[0], ssp[1], swr, smk)
    QSEM = {"sync": sx, "scalar": sq2, "gpsimd": sg}
    ESEM = {"tensor": pe, "scalar": act, "vector": dve, "gpsimd": pool}

    def dma(eng, out, in_, sem=None, **kw):
        s = QSEM[eng] if sem is None else sem
        v = s.inc(16)
        P.ev[eng].append(('i', s, 16))
        def f(e, out=out, in_=in_, kw=kw, h=s.h):
            e.dma_start(out=out, in_=in_, **kw).then_inc(h, 16)
        P.emit(eng, f)
        return v

    def op(eng, meth, *args, **kw):
        s = ESEM[eng]
        v = s.inc(1)
        P.ev[eng].append(('i', s, 1))
        def f(e, meth=meth, args=args, kw=kw, h=s.h):
            getattr(e, meth)(*args, **kw).then_inc(h, 1)
        P.emit(eng, f)
        return v

    def opl(eng, last, meth, *args, **kw):
        return (op if last else opn)(eng, meth, *args, **kw)

    def opn(eng, meth, *args, **kw):
        def f(e, meth=meth, args=args, kw=kw):
            getattr(e, meth)(*args, **kw)
        P.emit(eng, f)

    def fm_dram(ap):
        return ap.rearrange("(c p) t -> p c t", p=128)

    es_i = ExitStack()
    sbi = mk_sb(es_i)
    psi = mk_ps(es_i)

    dma("sync", tbl_s[:], tbl_d)
    dma("sync", identf[:], identf_d)
    dma("sync", Eb[:], Eb_d)
    dma("gpsimd", Ab[:], A_fmd)
    dma("gpsimd", Bwb[:], Bw_fmd)
    dma("gpsimd", rob[:], route_fmd)
    bvs = {k: sbi(f"bv{k}_s", [4, 1024], F32) for k in "qkv"}
    dma("sync", bvs["q"][:], bv_qd)
    dma("sync", bvs["k"][:], bv_kd)
    v_ld_bv = dma("sync", bvs["v"][:], bv_vd)

    opn("gpsimd", "memset", onescol[:], 1.0)
    opn("gpsimd", "memset", onesrow[:], 1.0)
    opn("gpsimd", "memset", ones1f[:], 1.0)
    v_rob = op("gpsimd", "memset", epscol[:], EPS)
    nc.const_aps.register(F32, EPS, epscol[0:1, 0:1])

    P.wait("scalar", sx, v_ld_bv)
    babs = {k: sbi(f"babs_{k}", [4, 1024], F32) for k in "qkv"}
    sgn = {k: sbi(f"sgn_{k}", [4, 1024], F32) for k in "qkv"}
    for k in "qkv":
        opn("scalar", "activation", babs[k][:], bvs[k][:], AF.Abs)
    for k in "qk":
        opn("scalar", "sign", sgn[k][:], bvs[k][:])
    v_bv = op("scalar", "sign", sgn["v"][:], bvs["v"][:])

    alph = {}
    tscr = sbi("tscr", [4, 512], F32)
    P.wait("vector", act, v_bv)
    P.wait("gpsimd", act, v_bv)
    for k in "qkv":
        a = sbi(f"alpha_{k}", [4, 1], F32)
        alph[k] = a
        width, step = 1024, 0
        cur = babs[k]
        while width > 1:
            half = width // 2
            eng = ("vector", "gpsimd")[step % 2]
            dst = tscr[:, 0:half] if half > 1 else a[:]
            vv = op(eng, "tensor_add", dst, cur[:, 0:half], cur[:, half:width])
            oth = ("vector", "gpsimd")[(step + 1) % 2]
            P.wait(oth, ESEM[eng], vv)
            cur = tscr
            width, step = half, step + 1
        P.wait("vector", ESEM[("vector", "gpsimd")[(step - 1) % 2]], vv)
        P.wait("gpsimd", ESEM[("vector", "gpsimd")[(step - 1) % 2]], vv)

    sqs = sbi("sqs", [4, 1024], F32)
    for k in "qkv":
        v1 = op("gpsimd", "tensor_mul", sqs[:], sgn[k][:], sgn[k][:])
        P.wait("vector", pool, v1)
        v1 = op("vector", "tensor_sub", sgn[k][:], sgn[k][:], sqs[:])
        P.wait("gpsimd", dve, v1)
        v1 = op("gpsimd", "tensor_scalar_add", sgn[k][:], sgn[k][:], 1.0)
        P.wait("vector", pool, v1)

    qkb_flat = sbi("qkb_flat", [4, 1024], F32)
    vb_flat = sbi("vb_flat", [4, 1024], F32)
    aqk = sbi("aqk", [4, 1], F32)
    v1 = op("vector", "tensor_mul", qkb_flat[:], sgn["q"][:], sgn["k"][:])
    v2 = op("vector", "tensor_mul", aqk[:], alph["q"][:], alph["k"][:])
    P.wait("gpsimd", dve, v2)
    v2 = op("gpsimd", "tensor_scalar_mul", aqk[:], aqk[:], 0.125 / (1024.0 * 1024.0))
    P.wait("vector", pool, v2)
    v_qkb = op("vector", "tensor_single_scalar", qkb_flat[:], qkb_flat[:],
               aqk[:], OP.mult)
    v2 = op("vector", "tensor_scalar_mul", alph["v"][:], alph["v"][:], 1.0 / 1024.0)
    P.wait("gpsimd", dve, v2)
    v_vbf = op("gpsimd", "tensor_single_scalar", vb_flat[:], sgn["v"][:],
               alph["v"][:], OP.mult)

    ps_i = psi("ps_i", [128, 512], F32)
    ps_r = psi("ps_r", [1, 512], F32)
    P.wait("tensor", dve, v_qkb)
    P.wait("tensor", pool, v_vbf)
    P.wait("tensor", sx)
    for cc in range(8):
        opn("tensor", "transpose", ps_i[:, 4 * cc:4 * cc + 4],
            qkb_flat[:, 128 * cc:128 * (cc + 1)], identf[0:4, 0:4])
    for cc in range(8):
        v_pe_t = opl("tensor", cc == 7, "transpose",
                     ps_i[:, 32 + 4 * cc:32 + 4 * cc + 4],
                     vb_flat[:, 128 * cc:128 * (cc + 1)], identf[0:4, 0:4])
    P.wait("scalar", pe, v_pe_t)
    for cc in range(8):
        opn("scalar", "copy", bass.AP(qkb_f, cc, [[32, 128], [8, 4]]),
            ps_i[:, 4 * cc:4 * cc + 4])
        v_qv_done = opl("scalar", cc == 7, "copy",
                        bass.AP(vb_f, cc, [[32, 128], [8, 4]]),
                        ps_i[:, 32 + 4 * cc:32 + 4 * cc + 4])

    P.wait("tensor", sg)
    P.wait("tensor", pool, v_rob)
    for c in range(NCH):
        v = opl("tensor", c == NCH - 1, "matmul", ps_r[0:1, 0:128], onescol[:],
                Ab[:, 128 * c:128 * (c + 1)], start=(c == 0), stop=(c == NCH - 1))
    P.wait("scalar", pe, v)
    v_as = op("scalar", "copy", Asum_b[:], ps_r[0:1, 0:128])
    P.wait("tensor", act, v_as)
    for c in range(NCH):
        v = opl("tensor", c == NCH - 1, "matmul", ps_r[0:1, 128:136], onescol[:],
                rob[:, 8 * c:8 * c + 8], start=(c == 0), stop=(c == NCH - 1))
    P.wait("scalar", pe, v)
    v_rs = op("scalar", "copy", Rsum_b[:], ps_r[0:1, 128:136])

    bws = sbi("bws", [128, 2048], F32)
    P.wait("vector", sg)
    vv = op("vector", "tensor_add", bws[:, 0:2048], Bwb[:, 0:2048], Bwb[:, 2048:4096])
    width, step = 2048, 1
    while width > 1:
        half = width // 2
        eng = ("vector", "gpsimd")[step % 2]
        P.wait(eng, ESEM[("vector", "gpsimd")[(step - 1) % 2]], vv)
        vv = op(eng, "tensor_add", bws[:, 0:half], bws[:, 0:half], bws[:, half:width])
        width, step = half, step + 1
    P.wait("scalar", ESEM[("vector", "gpsimd")[(step - 1) % 2]], vv)
    v_bws = op("scalar", "copy", BwSum_b[:], bws[:, 0:1])

    P.barrier(ALLSEMS)
    es_i.close()

    def alloc_prod(stack, with_xh, pfx=""):
        s0 = mk_sb(stack)
        p0 = mk_ps(stack)
        s = lambda n, sh, dt: s0(pfx + n, sh, dt)
        p = lambda n, sh, dt=F32: p0(pfx + n, sh, dt)
        B = NS()
        B.xf = [s("xf0", [128, NCH * TSS], F32), s("xf1", [128, NCH * TSS], F32)]
        B.xb = s("xb", [128, NCH * TSS], BF16)
        B.sqx = s("sqx", [128, NCH * TSS], BF16)
        B.x1f = s("x1f", [128, NCH * TSS], F32)
        B.xh = ([s("xh0", [128, NCH * TSS], BF16),
                 s("xh1", [128, NCH * TSS], BF16)] if with_xh else None)
        B.murow2 = s("murow2", [1, 2 * TSS], F32)
        B.murow_nb = s("murow_nb", [1, 2 * TSS], BF16)
        B.s2row = s("s2row", [1, 2 * TSS], F32)
        B.sfrow = s("sfrow", [1, 2 * TSS], F32)
        B.lsb = s("lsb", [8, 2 * TSS], F32)
        B.hb = s("hb_t", [128, 2 * TSS], BF16)
        B.musq = s("musq", [1, TSS], F32)
        B.var1 = s("var1", [1, TSS], F32)
        B.r1sq_row = s("r1sq_row", [1, TSS], F32)
        B.r1sq_col = s("r1sq_col", [128, 4], F32)
        B.m1sq = s("m1sq", [1, TSS], F32)
        B.ltok = s("ltok", [128, 16], F32)
        B.m4t = s("m4t", [128, 8], F32)
        B.m2t = s("m2t", [128, 4], F32)
        B.m1t = s("m1t", [128, 2], F32)
        B.difft = s("difft", [128, 16], F32)
        B.sgnt = s("sgnt", [128, 16], F32)
        B.rwt = s("rwt", [128, 32], F32)
        B.rw2t = s("rw2t", [128, 16], F32)
        B.rwT_b = s("rwT_b", [8, TSS], BF16)
        B.wh = s("wh_t", [128, TSS], BF16)
        B.ps_sx = p("ps_sx", [1, 512], F32)
        B.ps_sq = p("ps_sq", [1, 512], F32)
        B.ps_log = p("ps_log", [128, 512], F32)
        B.ps_h = p("ps_h", [128, 512], F32)
        B.ps_f = [p("ps_f0", [128, 512], F32), p("ps_f1", [128, 512], F32),
                  p("ps_f2", [128, 512], F32)]
        B.ps_rep = p("ps_rep", [128, 512], F32)
        B.M = {
            "xf_dve": [None, None], "xf_act": [None, None],
            "xb_rd": None, "sqx_rd": None,
            "xh_dma": [None, None], "v2row_dma": [None, None],
            "pssx_rd": None, "pssq_rd": None, "pslog_rd": None, "psh_rd": None,
            "psf_rd": [None, None, None], "pst_rd": None, "psrep_rd": None,
            "pslog_rd2": None, "psrep_act": None,
            "wh_rd": None, "sq1_rd": None, "x1f_dma": None, "cl_prev": None,
            "rw2t_pe": None, "s2w_dve": None, "s2row_rd": None, "m1sq_rd": None,
        }
        return B

    def stage_a(B, i, src, mode):
        M = B.M
        q = i % 2
        pc = q * TSS
        R0 = TSS * i
        xfq, xbq, sqq = B.xf[q], B.xb, B.sqx
        P.waitm("sync", M["xf_dve"][q])
        P.waitm("sync", M["xf_act"][q])
        if mode == "sub3":
            for g in range(2):
                v_ld = dma("sync",
                           xfq.rearrange("p (c t) -> p c t", t=TSS)[:, 8 * g:8 * g + 8, :],
                           fm_dram(src)[:, 8 * g:8 * g + 8, R0:R0 + TSS],
                           sem=sld[q])
            P.waitm("scalar", M["xf_dve"][q])
            P.waitm("scalar", M["xf_act"][q])
            for g in range(2, 4):
                v_ld2 = dma("scalar",
                            xfq.rearrange("p (c t) -> p c t", t=TSS)[:, 8 * g:8 * g + 8, :],
                            fm_dram(src)[:, 8 * g:8 * g + 8, R0:R0 + TSS],
                            sem=sld2[q])
        else:
            half = NCH * TSS // 2
            v_ld = dma("sync", xfq[:, 0:half],
                       src[:, NCH * TSS * i:NCH * TSS * i + half], sem=sld[q])
            P.waitm("scalar", M["xf_dve"][q])
            P.waitm("scalar", M["xf_act"][q])
            v_ld2 = dma("scalar", xfq[:, half:2 * half],
                        src[:, NCH * TSS * i + half:NCH * TSS * (i + 1)],
                        sem=sld2[q])
        P.waitm("gpsimd", M["xb_rd"])
        if mode == "sub3":
            for g in range(4):
                v_xb = dma("gpsimd",
                           xbq.rearrange("p (c t) -> p c t", t=TSS)[:, 8 * g:8 * g + 8, :],
                           fm_dram(src)[:, 8 * g:8 * g + 8, R0:R0 + TSS])
        else:
            v_xb = dma("gpsimd", xbq[:], src[:, NCH * TSS * i:NCH * TSS * (i + 1)])
        P.wait("scalar", sld[q], v_ld)
        P.wait("scalar", sld2[q], v_ld2)
        P.waitm("scalar", M["sqx_rd"])
        P.waitm("scalar", M["sq1_rd"])
        v_sq = op("scalar", "activation", sqq[:], xfq[:], AF.Square)
        P.wait("tensor", sg, v_xb)
        P.waitm("tensor", M["pssx_rd"])
        for c in range(NCH):
            v_g0 = opl("tensor", c == NCH - 1, "matmul", B.ps_sx[0:1, 0:TSS],
                       onescol[:], xbq[:, TSS * c:TSS * (c + 1)],
                       start=(c == 0), stop=(c == NCH - 1))
        P.wait("tensor", act, v_sq)
        P.waitm("tensor", M["pssq_rd"])
        for c in range(NCH):
            v_g2 = opl("tensor", c == NCH - 1, "matmul", B.ps_sq[0:1, 0:TSS],
                       onescol[:], sqq[:, TSS * c:TSS * (c + 1)],
                       start=(c == 0), stop=(c == NCH - 1))
        P.wait("scalar", pe, v_g0)
        v_mu = op("scalar", "activation", B.murow2[0:1, pc:pc + TSS],
                  B.ps_sx[0:1, 0:TSS], AF.Copy, scale=-1.0 / D)
        v_mub = op("scalar", "activation", B.murow_nb[0:1, pc:pc + TSS],
                   B.ps_sx[0:1, 0:TSS], AF.Copy, scale=-1.0 / D)
        M["pssx_rd"] = (act, v_mub)
        P.waitm("tensor", M["pslog_rd"])
        for c in range(NCH):
            opn("tensor", "matmul", B.ps_log[0:8, 0:TSS], rob[:, 8 * c:8 * (c + 1)],
                xbq[:, TSS * c:TSS * (c + 1)], start=(c == 0), stop=False)
        P.waitm("tensor", M["psh_rd"])
        for c in range(NCH):
            v_g3m = opl("tensor", c == NCH - 1, "matmul", B.ps_h[:, 0:TSS],
                        Ab[:, 128 * c:128 * (c + 1)],
                        xbq[:, TSS * c:TSS * (c + 1)], start=(c == 0), stop=False)
        P.wait("tensor", act, v_mub)
        v_g1c = op("tensor", "matmul", B.ps_log[0:8, 0:TSS], Rsum_b[:],
                   B.murow_nb[0:1, pc:pc + TSS], start=False, stop=True)
        v_g3c = op("tensor", "matmul", B.ps_h[:, 0:TSS], Asum_b[:],
                   B.murow_nb[0:1, pc:pc + TSS], start=False, stop=True)
        M["xb_rd"] = (pe, v_g3c)
        M["sqx_rd"] = (pe, v_g2)
        P.wait("scalar", pe, v_g2)
        P.waitm("scalar", M["s2w_dve"])
        v_s2 = op("scalar", "activation", B.s2row[0:1, pc:pc + TSS],
                  B.ps_sq[0:1, 0:TSS], AF.Copy, scale=1.0 / D, bias=EPS)
        M["pssq_rd"] = (act, v_s2)
        P.wait("scalar", pe, v_g1c)
        v_lsb = op("scalar", "copy", B.lsb[0:8, pc:pc + TSS], B.ps_log[0:8, 0:TSS])
        return dict(i=i, q=q, pc=pc, R0=R0, v_sq=v_sq, v_s2=v_s2, v_mu=v_mu,
                    v_lsb=v_lsb, v_g3c=v_g3c)

    def a2_head(B, ctx):
        M = B.M
        pc = ctx["pc"]
        v_musq = op("scalar", "activation", B.musq[:], B.murow2[0:1, pc:pc + TSS],
                    AF.Square)
        P.wait("gpsimd", act, v_musq)
        P.wait("gpsimd", act, ctx["v_s2"])
        P.waitm("gpsimd", M["v2row_dma"][0])
        vv = op("gpsimd", "tensor_sub", B.var1[:], B.s2row[0:1, pc:pc + TSS],
                B.musq[:])
        M["s2row_rd"] = (pool, vv)
        P.wait("vector", pool, vv)
        ctx["v_r1"] = op("vector", "reciprocal", B.r1sq_row[:], B.var1[:])

    def a2_mid(B, ctx):
        M = B.M
        q = ctx["q"]
        P.wait("tensor", dve, ctx["v_r1"])
        P.waitm("tensor", M["pslog_rd"])
        P.waitm("tensor", M["pslog_rd2"])
        for tt in range(2):
            v_tr = opl("tensor", tt == 1, "transpose", B.ps_log[:, 256 + tt:257 + tt],
                       B.r1sq_row[0:1, 128 * tt:128 * (tt + 1)], identf[0:1, 0:1])
        P.wait("scalar", pe, v_tr)
        v_r1c = op("scalar", "copy", B.r1sq_col[:, 2 * q:2 * q + 2],
                   B.ps_log[:, 256:258])
        M["pslog_rd"] = (act, v_r1c)
        ctx["v_r1c"] = v_r1c

    def a2_tail(B, ctx):
        M = B.M
        i, q, pc = ctx["i"], ctx["q"], ctx["pc"]
        P.wait("tensor", act, ctx["v_lsb"])
        P.wait("tensor", act, ctx["v_r1c"])
        for tt in range(2):
            v_trl = opl("tensor", tt == 1, "transpose",
                        B.ps_log[:, 264 + 8 * tt:272 + 8 * tt],
                        B.lsb[0:8, pc + 128 * tt:pc + 128 * (tt + 1)],
                        identf[0:8, 0:8])
        P.wait("scalar", pe, ctx["v_g3c"])
        v_hb = op("scalar", "copy", B.hb[:, pc:pc + TSS], B.ps_h[:, 0:TSS])
        M["psh_rd"] = (act, v_hb)
        P.wait("vector", pe, v_trl)
        P.waitm("vector", M["cl_prev"])
        v_lt = op("vector", "tensor_copy", B.ltok[:], B.ps_log[:, 264:280])
        M["pslog_rd2"] = (dve, v_lt)
        P.wait("vector", dve, v_lt)
        opn("vector", "tensor_max", B.m4t[:, 0:4], B.ltok[:, 0:4], B.ltok[:, 4:8])
        v4 = op("vector", "tensor_max", B.m4t[:, 4:8], B.ltok[:, 8:12], B.ltok[:, 12:16])
        P.wait("vector", dve, v4)
        opn("vector", "tensor_max", B.m2t[:, 0:2], B.m4t[:, 0:2], B.m4t[:, 2:4])
        v2 = op("vector", "tensor_max", B.m2t[:, 2:4], B.m4t[:, 4:6], B.m4t[:, 6:8])
        P.wait("vector", dve, v2)
        opn("vector", "tensor_max", B.m1t[:, 0:1], B.m2t[:, 0:1], B.m2t[:, 1:2])
        vv = op("vector", "tensor_max", B.m1t[:, 1:2], B.m2t[:, 2:3], B.m2t[:, 3:4])
        P.wait("gpsimd", dve, vv)
        opn("gpsimd", "tensor_single_scalar", B.difft[:, 0:8], B.ltok[:, 0:8],
            B.m1t[:, 0:1], OP.subtract)
        vv = op("gpsimd", "tensor_single_scalar", B.difft[:, 8:16], B.ltok[:, 8:16],
                B.m1t[:, 1:2], OP.subtract)
        P.wait("scalar", pool, vv)
        v_sg2 = op("scalar", "sign", B.sgnt[:], B.difft[:])
        P.wait("vector", act, v_sg2)
        v_rw = op("vector", "scalar_tensor_tensor", B.rwt[:, 16 * q:16 * q + 16],
                  B.sgnt[:], 1.0, B.ltok[:], op0=OP.add, op1=OP.mult)
        M["cl_prev"] = (dve, v_rw)
        ctx["v_rw"] = v_rw


    def b_head(B, ctx):
        M = B.M
        q = ctx["q"]
        P.wait("gpsimd", dve, ctx["v_rw"])
        P.wait("gpsimd", act, ctx["v_r1c"])
        P.waitm("gpsimd", M["rw2t_pe"])
        opn("gpsimd", "tensor_single_scalar", B.rw2t[:, 0:8],
            B.rwt[:, 16 * q:16 * q + 8], B.r1sq_col[:, 2 * q:2 * q + 1], OP.mult)
        v_rw2 = op("gpsimd", "tensor_single_scalar", B.rw2t[:, 8:16],
                   B.rwt[:, 16 * q + 8:16 * q + 16],
                   B.r1sq_col[:, 2 * q + 1:2 * q + 2], OP.mult)
        P.wait("tensor", pool, v_rw2)
        P.waitm("tensor", M["psrep_rd"])
        P.waitm("tensor", M["psrep_act"])
        for tt in range(2):
            v_trw = opl("tensor", tt == 1, "transpose",
                        B.ps_rep[0:8, 128 + 128 * tt:256 + 128 * tt],
                        B.rw2t[:, 8 * tt:8 * (tt + 1)], identf[:, :])
        P.wait("scalar", pe, v_trw)
        M["rw2t_pe"] = (pe, v_trw)
        v_rwT = op("scalar", "copy", B.rwT_b[0:8, 0:TSS],
                   B.ps_rep[0:8, 128:128 + TSS])
        M["psrep_act"] = (act, v_rwT)
        ctx["v_rwT"] = v_rwT

    def b_mid(B, ctx):
        M = B.M
        pc = ctx["pc"]
        P.wait("tensor", act, ctx["v_rwT"])
        v_rwe = op("tensor", "matmul", B.ps_rep[:, 0:TSS], Eb[:],
                   B.rwT_b[0:8, 0:TSS], start=True, stop=True)
        P.wait("vector", pe, v_rwe)
        P.waitm("vector", M["wh_rd"])
        v_wh = op("vector", "tensor_tensor", B.wh[:], B.hb[:, pc:pc + TSS],
                  B.ps_rep[:, 0:TSS], op=OP.mult)
        ctx["v_wh"] = v_wh

    def b_bulk(B, ctx):
        M = B.M
        q = ctx["q"]
        xfq = B.xf[q]
        P.wait("tensor", dve, ctx["v_wh"])
        ev = [M["psf_rd"][0], M["psf_rd"][1], M["psf_rd"][2]]
        for k in range(16):
            par = k % 3
            P.waitm("tensor", ev[par])
            for half in range(2):
                oc = 2 * k + half
                v_fp = opl("tensor", half == 1, "matmul",
                           B.ps_f[par][:, TSS * half:TSS * (half + 1)],
                           Bwb[:, 128 * oc:128 * (oc + 1)], B.wh[:],
                           start=True, stop=True)
            P.wait("vector", pe, v_fp)
            if k == 0:
                P.waitm("vector", M["x1f_dma"])
            sl = slice(2 * k * TSS, (2 * k + 2) * TSS)
            v_ev = op("vector", "scalar_tensor_tensor", B.x1f[:, sl],
                      B.ps_f[par][:, 0:2 * TSS], 0.5, xfq[:, sl],
                      op0=OP.mult, op1=OP.add)
            ev[par] = (dve, v_ev)
        M["psf_rd"] = ev
        ctx["last_g4"] = v_fp
        ctx["v_evall"] = max(x[1] for x in ev)

    def b_tail(B, ctx, src, mode):
        M = B.M
        i, q, pc, R0 = ctx["i"], ctx["q"], ctx["pc"], ctx["R0"]
        xfq = B.xf[q]
        v_wh, last_g4, v_evall = ctx["v_wh"], ctx["last_g4"], ctx["v_evall"]
        if mode == "sub3":
            P.wait("sync", dve, v_evall)
            for g in range(4):
                v_out = dma("sync",
                            fm_dram(out_fm)[:, 8 * g:8 * g + 8, R0:R0 + TSS],
                            B.x1f.rearrange("p (c t) -> p c t", t=TSS)[:, 8 * g:8 * g + 8, :],
                            sem=swr)
            M["x1f_dma"] = (swr, v_out)
            M["wh_rd"] = (pe, last_g4)
            M["psrep_rd"] = (dve, v_wh)
            M["xf_dve"][q] = (dve, v_evall)
            M["xf_act"][q] = (act, ctx["v_sq"])
            return v_out
        v_g5 = op("tensor", "matmul", B.ps_rep[0:1, 256:256 + TSS], BwSum_b[:],
                  B.wh[:], start=True, stop=True)
        M["wh_rd"] = (pe, v_g5)
        P.wait("scalar", pe, v_g5)
        v_sf = op("scalar", "activation", B.sfrow[0:1, pc:pc + TSS],
                  B.ps_rep[0:1, 256:256 + TSS], AF.Copy, scale=-0.5 / D)
        M["psrep_act"] = (act, v_sf)
        P.wait("tensor", dve, v_wh)
        P.wait("tensor", act, v_sf)
        opn("tensor", "matmul", B.ps_rep[:, 0:TSS], ones1f[:],
            B.murow2[0:1, pc:pc + TSS], start=True, stop=False)
        v_g7 = op("tensor", "matmul", B.ps_rep[:, 0:TSS], ones1f[:],
                  B.sfrow[0:1, pc:pc + TSS], start=False, stop=True)
        P.wait("vector", pe, v_g7)
        rep_bc = B.ps_rep[:, 0:TSS].unsqueeze(1).broadcast_to([128, NCH, TSS])
        if mode == "pre":
            xhq = B.xh[q]
            P.waitm("vector", M["xh_dma"][q])
            P.wait("vector", dve, v_evall)
            v_ctr = op("vector", "tensor_tensor",
                       xhq.rearrange("p (c t) -> p c t", t=TSS),
                       B.x1f.rearrange("p (c t) -> p c t", t=TSS), rep_bc,
                       op=OP.add)
            M["psrep_rd"] = (dve, v_ctr)
            P.wait("sync", dve, v_ctr)
            for g in range(4):
                v_sp = dma("sync",
                           fm_dram(spill_xh)[:, 8 * g:8 * g + 8, R0:R0 + TSS],
                           xhq.rearrange("p (c t) -> p c t", t=TSS)[:, 8 * g:8 * g + 8, :],
                           sem=ssp[q])
            M["xh_dma"][q] = (ssp[q], v_sp)
            M["xf_dve"][q] = (dve, v_evall)
            M["xf_act"][q] = (act, ctx["v_sq"])
            M["psrep_rd"] = (dve, v_ctr)
            ctx["v_evall"] = v_evall
            ctx["v_ctr"] = v_ctr
            return v_sp
        P.wait("sync", dve, v_evall)
        for g in range(4):
            v_spx1 = dma("sync",
                         fm_dram(spill_x1own)[:, 8 * g:8 * g + 8, R0:R0 + TSS],
                         B.x1f.rearrange("p (c t) -> p c t", t=TSS)[:, 8 * g:8 * g + 8, :],
                         sem=swr)
        P.wait("vector", dve, v_evall)
        v_ctr = op("vector", "tensor_tensor",
                   xfq.rearrange("p (c t) -> p c t", t=TSS),
                   B.x1f.rearrange("p (c t) -> p c t", t=TSS), rep_bc, op=OP.add)
        M["psrep_rd"] = (dve, v_ctr)
        P.wait("vector", swr, v_spx1)
        P.wait("vector", dve, v_ctr)
        qkb_bc = qkb_f[:].unsqueeze(2).broadcast_to([128, NCH, TSS])
        v_qt = op("vector", "tensor_tensor",
                  B.x1f.rearrange("p (c t) -> p c t", t=TSS),
                  xfq.rearrange("p (c t) -> p c t", t=TSS), qkb_bc, op=OP.mult)
        P.waitm("vector", M["xh_dma"][q])
        P.wait("vector", dve, v_qt)
        rep_own = rstdownrep[:, R0:R0 + TSS].unsqueeze(1).broadcast_to(
            [128, NCH, TSS])
        v_qb = op("vector", "tensor_tensor",
                  B.xh[q].rearrange("p (c t) -> p c t", t=TSS),
                  B.x1f.rearrange("p (c t) -> p c t", t=TSS), rep_own, op=OP.mult)
        P.wait("sync", dve, v_qb)
        for g in range(4):
            v_spq = dma("sync",
                        fm_dram(spill_q)[:, 8 * g:8 * g + 8, R0:R0 + TSS],
                        B.xh[q].rearrange("p (c t) -> p c t", t=TSS)[:, 8 * g:8 * g + 8, :],
                        sem=ssp[q])
        M["xh_dma"][q] = (ssp[q], v_spq)
        M["xf_dve"][q] = (dve, v_qt)
        M["xf_act"][q] = (act, ctx["v_sq"])
        M["x1f_dma"] = (swr, v_spx1)
        return v_spq

    def bdef_sq(B, ctx):
        M = B.M
        P.wait("scalar", dve, ctx["v_evall"])
        P.waitm("scalar", M["sqx_rd"])
        ctx["v_sq1"] = op("scalar", "activation", B.sqx[:], B.x1f[:], AF.Square)

    def bdef_rest(B, ctx):
        M = B.M
        q, pc, R0 = ctx["q"], ctx["pc"], ctx["R0"]
        ev = M["psf_rd"]
        P.wait("tensor", act, ctx["v_sq1"])
        P.waitm("tensor", ev[0])
        for c in range(NCH):
            v_g6 = opl("tensor", c == NCH - 1, "matmul",
                       B.ps_f[0][0:1, 0:TSS], onescol[:],
                       B.sqx[:, TSS * c:TSS * (c + 1)], start=(c == 0),
                       stop=(c == NCH - 1))
        M["sq1_rd"] = (pe, v_g6)
        P.wait("scalar", pe, v_g6)
        P.waitm("scalar", M["s2row_rd"])
        v_s2b = op("scalar", "activation", B.s2row[0:1, pc:pc + TSS],
                   B.ps_f[0][0:1, 0:TSS], AF.Copy, scale=1.0 / D)
        M["psf_rd"][0] = (act, v_s2b)
        P.waitm("scalar", M["m1sq_rd"])
        v_m1sq = op("scalar", "activation", B.m1sq[:], B.ps_rep[0:1, 0:TSS],
                    AF.Square)
        M["psrep_act"] = (act, v_m1sq)
        P.wait("vector", act, v_m1sq)
        P.wait("vector", act, v_s2b)
        P.waitm("vector", M["v2row_dma"][0])
        v_v2 = op("vector", "tensor_sub", B.var1[:],
                  B.s2row[0:1, pc:pc + TSS], B.m1sq[:])
        M["s2w_dve"] = (dve, v_v2)
        M["m1sq_rd"] = (dve, v_v2)
        P.wait("sync", dve, v_v2)
        v_v2d = dma("sync", var2d[0:1, R0:R0 + TSS], B.var1[:], sem=sv2[q])
        M["v2row_dma"][0] = (sv2[q], v_v2d)

    def run_pipeline(B, n, src, mode):
        prev = None
        for i in range(n):
            c = stage_a(B, i, src, mode)
            if prev is not None:
                b_head(B, prev)
            a2_head(B, c)
            if prev is not None:
                b_mid(B, prev)
            a2_mid(B, c)
            a2_tail(B, c)
            if prev is not None:
                b_bulk(B, prev)
                b_tail(B, prev, src, mode)
                if mode == "pre":
                    bdef_sq(B, prev)
                    bdef_rest(B, prev)
            prev = c
        b_head(B, prev)
        b_mid(B, prev)
        b_bulk(B, prev)
        b_tail(B, prev, src, mode)
        if mode == "pre":
            bdef_sq(B, prev)
            bdef_rest(B, prev)

    es_p = ExitStack()
    Bp = alloc_prod(es_p, with_xh=True)
    P.wait("tensor", act, v_as)
    P.wait("tensor", act, v_rs)
    P.wait("tensor", act, v_bws)
    P.wait("tensor", act, v_qv_done)
    run_pipeline(Bp, NSS, x_fm, "pre")

    sbp = mk_sb(es_p)
    v72 = sbp("v72", [72, 128], F32)
    rstdownrow_b = sbp("rstdownrow_b", [1, OWN], BF16)
    rstdownrep = sbp("rstdownrep", [128, OWN], BF16)
    P.wait("sync", sv2[0])
    P.wait("sync", sv2[1])
    v_v72 = dma("sync", v72[:], var2d.rearrange("a (r c) -> (a r) c", c=128))
    P.wait("scalar", sx, v_v72)
    v_ln = op("scalar", "activation", v72[:], v72[:], AF.Ln, bias=epscol[0:72, 0:1])
    P.wait("scalar", act, v_ln)
    v_ex = op("scalar", "activation", v72[:], v72[:], AF.Exp, scale=-0.5)
    P.wait("sync", act, v_ex)
    v_rdd = dma("sync", rstdd.rearrange("a (r c) -> (a r) c", c=128), v72[:])
    P.wait("tensor", act, v_ex)
    P.waitm("tensor", Bp.M["psrep_rd"])
    P.waitm("tensor", Bp.M["psrep_act"])
    v = op("tensor", "transpose", Bp.ps_rep[:, 0:72], v72[:], identf[0:72, 0:72])
    P.wait("scalar", pe, v)
    v_rc = op("scalar", "copy", rstdcols[:], Bp.ps_rep[:, 0:72])
    Bp.M["psrep_act"] = (act, v_rc)

    def f_ownrow(e):
        r = e.alloc_register("rofs")
        e.reg_load(r, tbl_s[0:1, 48:49])
        e.dma_start(out=rstdownrow_b[0:1, 0:1024],
                    in_=bass.AP(rstdd_t, r, [[PRE, 1], [1, 1024]])
                    ).then_inc(sg.h, 16)
    P.wait("gpsimd", sx, v_rdd)
    P.emit("gpsimd", f_ownrow)
    v_orA = sg.inc(16)
    P.ev["gpsimd"].append(('i', sg, 16))
    v_orB = dma("gpsimd", rstdownrow_b[0:1, 1024:2048],
                rstdd[0:1, PRE - 1024:PRE])
    P.ev["gpsimd"].append(('w', sg, v_orB))
    P.emit("gpsimd", lambda e, v=v_orB: e.wait_ge(sg.h, v))
    P.wait("tensor", sg, v_orB)
    P.waitm("tensor", Bp.M["psrep_rd"])
    P.waitm("tensor", Bp.M["psrep_act"])
    v_orr = None
    for seg in range(4):
        if v_orr is not None:
            P.wait("tensor", act, v_orr)
        v = op("tensor", "matmul", Bp.ps_rep[:, 0:512], onesrow[:],
               rstdownrow_b[0:1, 512 * seg:512 * (seg + 1)], start=True, stop=True)
        P.wait("scalar", pe, v)
        v_orr = op("scalar", "copy", rstdownrep[:, 512 * seg:512 * (seg + 1)],
                   Bp.ps_rep[:, 0:512])
    Bp.M["psrep_rd"] = (act, v_orr)

    run_pipeline(Bp, NOSS, x_own, "own")

    P.barrier(ALLSEMS)
    es_p.close()

    es_a = ExitStack()
    sba = mk_sb(es_a)
    psa = mk_ps(es_a)
    Kring = [sba("Kring0", [128, 8 * KC], BF16), sba("Kring1", [128, 8 * KC], BF16)]
    Vring = [sba("Vring0", [128, 8 * KC], BF16), sba("Vring1", [128, 8 * KC], BF16)]
    Qcur = sba("Qcur", [128, 8 * KC], BF16)
    Osb = sba("Osb", [128, 8 * KC], F32)
    Pst = sba("Pst", [128, 16 * 512], BF16)
    sigt = sba("sigt", [128, 4 * 512], BF16)
    maskcur = sba("maskcur", [128, 16 * 512], BF16)
    ps_S = [psa("ps_S0", [128, 1024], F32), psa("ps_S1", [128, 1024], F32)]
    ps_O = [psa("ps_O0", [128, 1024], F32), psa("ps_O1", [128, 1024], F32)]

    spxh = fm_dram(spill_xh)
    spq = fm_dram(spill_q)
    spx1 = fm_dram(spill_x1own)
    spx2 = fm_dram(spill_x2)

    def cond_dma(out, in_, tbl_idx, sem=None, **kw):
        s = sg if sem is None else sem
        v = s.inc(16)
        P.ev["gpsimd"].append(('i', s, 16))
        def f(e, out=out, in_=in_, idx=tbl_idx, kw=kw, h=s.h):
            cv = e.value_load(tbl_s[0:1, idx:idx + 1], min_val=0, max_val=1)
            e.dma_start(out=out, in_=in_, cond=cv, **kw).then_inc(h, 16)
        P.emit("gpsimd", f)
        return v

    AM = {"K_pe": [0, 0], "V_pe": [0, 0], "Q_pe": 0, "mask_pool": 0,
          "O_dve": 0, "sig_rd": [0] * 4, "Pst_rd": 0,
          "psS_rd": [0, 0], "psO_rd": [0, 0], "O_dump": 0}

    for p in range(NPASS):
        P.wait("gpsimd", dve, AM["O_dve"])
        P.wait("gpsimd", sx, AM["O_dump"])
        dma("gpsimd", Osb.rearrange("p (c t) -> p c t", t=KC),
            spx1[:, 8 * p:8 * p + 8, 0:1024])
        P.wait("gpsimd", pe, AM["Q_pe"])
        v_pass = dma("gpsimd", Qcur.rearrange("p (c t) -> p c t", t=KC),
                     spq[:, 8 * p:8 * p + 8, 0:1024])
        P.ev["gpsimd"].append(('w', sg, v_pass))
        P.emit("gpsimd", lambda e, v=v_pass: e.wait_ge(sg.h, v))
        for j in range(NSLOT):
            q = j % 2
            P.wait("gpsimd", dve, AM["O_dve"])
            v_dumpA = cond_dma(spx2[:, 8 * p:8 * p + 8, 0:1024],
                               Osb.rearrange("p (c t) -> p c t", t=KC), j)
            P.ev["gpsimd"].append(('w', sg, v_dumpA))
            P.emit("gpsimd", lambda e, v=v_dumpA: e.wait_ge(sg.h, v))
            v_ob = cond_dma(Osb.rearrange("p (c t) -> p c t", t=KC),
                            spx1[:, 8 * p:8 * p + 8, 1024:2048], j)
            P.wait("gpsimd", pe, AM["Q_pe"])
            v_qb = cond_dma(Qcur.rearrange("p (c t) -> p c t", t=KC),
                            spq[:, 8 * p:8 * p + 8, 1024:2048], j)
            P.wait("gpsimd", dve, AM["mask_pool"])
            cond_dma(maskcur[:], mask_tri_d, 16 + j, sem=smk)
            v_mload = cond_dma(maskcur[:], mask_ones_d, 32 + j, sem=smk)
            P.wait("sync", pe, AM["K_pe"][q])
            v_k = dma("sync", Kring[q].rearrange("p (c t) -> p c t", t=KC),
                      spxh[:, 8 * p:8 * p + 8, KC * j:KC * (j + 1)],
                      sem=skk[q])
            P.wait("scalar", pe, AM["V_pe"][q])
            v_v = None
            for sbk in range(8):
                v_v = dma("scalar", Vring[q][:, KC * sbk:KC * (sbk + 1)],
                          spill_xh[1024 * p:1024 * (p + 1),
                                   KC * j + 128 * sbk:KC * j + 128 * (sbk + 1)],
                          transpose=True, sem=svv[q])
            for sbk in range(8):
                par = sbk % 2
                P.wait("tensor", skk[q], v_k)
                P.wait("tensor", sg, v_qb)
                P.wait("tensor", dve, AM["psS_rd"][par])
                for dc in range(8):
                    for th in range(2):
                        vx = opl("tensor", dc == 7 and th == 1, "matmul",
                                 ps_S[par][:, 512 * th:512 * (th + 1)],
                                 Kring[q][:, KC * dc + 128 * sbk:KC * dc + 128 * (sbk + 1)],
                                 Qcur[:, KC * dc + 512 * th:KC * dc + 512 * (th + 1)],
                                 start=(dc == 0), stop=(dc == 7))
                v_s = vx
                AM["Q_pe"] = v_s
                AM["K_pe"][q] = v_s
                P.wait("scalar", pe, v_s)
                P.wait("vector", smk, v_mload)
                for th in range(2):
                    rot = (2 * sbk + th) % 4
                    P.wait("scalar", dve, AM["sig_rd"][rot])
                    v_sig = op("scalar", "activation",
                               sigt[:, 512 * rot:512 * (rot + 1)],
                               ps_S[par][:, 512 * th:512 * (th + 1)], AF.Sigmoid,
                               scale=rstdcols[:, 8 * j + sbk:8 * j + sbk + 1])
                    P.wait("vector", act, v_sig)
                    P.wait("vector", pe, AM["Pst_rd"])
                    v_pp = op("vector", "scalar_tensor_tensor",
                              Pst[:, 512 * (2 * sbk + th):512 * (2 * sbk + th + 1)],
                              sigt[:, 512 * rot:512 * (rot + 1)],
                              rstdcols[:, 8 * j + sbk:8 * j + sbk + 1],
                              maskcur[:, 512 * (2 * sbk + th):512 * (2 * sbk + th + 1)],
                              op0=OP.mult, op1=OP.mult)
                    AM["sig_rd"][rot] = v_pp
                AM["psS_rd"][par] = v_pp
            AM["mask_pool"] = v_pp
            for oc in range(8):
                par = oc % 2
                P.wait("tensor", dve, v_pp)
                P.wait("tensor", svv[q], v_v)
                P.wait("tensor", dve, AM["psO_rd"][par])
                for sbk in range(8):
                    for th in range(2):
                        vx = opl("tensor", sbk == 7 and th == 1, "matmul",
                                 ps_O[par][:, 512 * th:512 * (th + 1)],
                                 Vring[q][:, KC * sbk + 128 * oc:KC * sbk + 128 * (oc + 1)],
                                 Pst[:, 512 * (2 * sbk + th):512 * (2 * sbk + th + 1)],
                                 start=(sbk == 0), stop=(sbk == 7))
                v_av = vx
                AM["V_pe"][q] = v_av
                AM["Pst_rd"] = v_av
                P.wait("vector", pe, v_av)
                P.wait("vector", sg, v_qb)
                for th in range(2):
                    sl = slice(KC * oc + 512 * th, KC * oc + 512 * (th + 1))
                    v_ev = op("vector", "scalar_tensor_tensor", Osb[:, sl],
                              ps_O[par][:, 512 * th:512 * (th + 1)],
                              vb_f[:, 8 * p + oc:8 * p + oc + 1],
                              Osb[:, sl], op0=OP.mult, op1=OP.add)
                AM["psO_rd"][par] = v_ev
                AM["O_dve"] = v_ev
        P.wait("sync", dve, AM["O_dve"])
        v_dump = dma("sync", spx2[:, 8 * p:8 * p + 8, 1024:2048],
                     Osb.rearrange("p (c t) -> p c t", t=KC))
        AM["O_dump"] = v_dump

    P.barrier(ALLSEMS)
    es_a.close()

    es_s = ExitStack()
    Bs = alloc_prod(es_s, with_xh=False, pfx="s_")
    run_pipeline(Bs, NOSS, spill_x2, "sub3")
    for s_ in ALLSEMS:
        if s_ not in (pe, act, dve, pool):
            P.wait("sync", s_)
    P.wait("gpsimd", swr)

    P.check()
    P.play(nc)
    es_s.close()
    es.close()
    nc.compile()
    return nc


def _stage(inputs):
    x = np.ascontiguousarray(inputs["L0_attn_x"])
    A = inputs["L0_ffn_A"]
    Bw = inputs["L0_ffn_Bw"]
    route = inputs["L0_ffn_route_w"]
    A_flat = A.reshape(128, D)
    A_fm = np.ascontiguousarray(
        A_flat.reshape(128, NCH, 128).transpose(2, 1, 0).reshape(128, NCH * 128))
    Bw_fm = np.ascontiguousarray(Bw.transpose(0, 2, 1).reshape(128, D))
    route_fm = np.ascontiguousarray(
        route.reshape(8, NCH, 128).transpose(2, 1, 0).reshape(128, NCH * 8))
    identf = np.eye(128, dtype=np.float32)
    Emat = np.zeros((8, 128), dtype=BF)
    for e in range(8):
        Emat[e, 16 * e:16 * (e + 1)] = 1
    s_i = np.arange(128)[:, None]
    t_i = np.arange(512)[None, :]
    mtri = np.zeros((128, 16 * 512), dtype=BF)
    for sbk in range(8):
        for th in range(2):
            m = ((sbk * 128 + s_i) <= (th * 512 + t_i)).astype(np.float32)
            mtri[:, 512 * (2 * sbk + th):512 * (2 * sbk + th + 1)] = m.astype(BF)
    mones = np.ones((128, 16 * 512), dtype=BF)

    xT = [np.ascontiguousarray(x[b].T) for b in range(2)]
    in_maps = []
    for c in range(8):
        cA = c + 1
        cB = 8 - c
        xc = np.concatenate([xT[0][:, :1024 * cA], xT[1][:, :1024 * cB]], axis=1)
        xo = np.concatenate([xT[0][:, 1024 * c:1024 * (c + 1)],
                             xT[1][:, 1024 * (7 - c):1024 * (8 - c)]], axis=1)
        xc = xc.reshape(NCH, 128, NSS, TSS).transpose(1, 2, 0, 3).reshape(
            128, NSS * NCH * TSS)
        xo = xo.reshape(NCH, 128, NOSS, TSS).transpose(1, 2, 0, 3).reshape(
            128, NOSS * NCH * TSS)
        tbl = np.zeros((1, 64), np.int32)
        tri_prev = False
        for j in range(NSLOT):
            tbl[0, j] = 1 if j == cA else 0
            tri = (j == cA - 1) or (j == 8)
            tbl[0, 16 + j] = 1 if (tri and not tri_prev) else 0
            tbl[0, 32 + j] = 1 if ((not tri) and (j == 0 or tri_prev)) else 0
            tri_prev = tri
        tbl[0, 48] = (cA - 1) * 1024
        in_maps.append({
            "x_fm": np.ascontiguousarray(xc),
            "x_own": np.ascontiguousarray(xo),
            "A_fm": A_fm, "Bw_fm": Bw_fm, "route_fm": route_fm,
            "bv_q": inputs["L0_attn_bv_q"], "bv_k": inputs["L0_attn_bv_k"],
            "bv_v": inputs["L0_attn_bv_v"],
            "identf": identf, "mask_tri": mtri, "mask_ones": mones,
            "E_b": Emat, "tbl": tbl,
        })
    return in_maps


def kernel(L0_attn_x, L0_attn_bv_q, L0_attn_bv_k, L0_attn_bv_v,
           L0_ffn_route_w, L0_ffn_A, L0_ffn_Bw):
    inputs = dict(L0_attn_x=np.asarray(L0_attn_x, np.float32),
                  L0_attn_bv_q=np.asarray(L0_attn_bv_q, np.float32),
                  L0_attn_bv_k=np.asarray(L0_attn_bv_k, np.float32),
                  L0_attn_bv_v=np.asarray(L0_attn_bv_v, np.float32),
                  L0_ffn_route_w=np.asarray(L0_ffn_route_w, np.float32),
                  L0_ffn_A=np.asarray(L0_ffn_A, np.float32),
                  L0_ffn_Bw=np.asarray(L0_ffn_Bw, np.float32))
    in_maps = _stage(inputs)
    nc = build_program()
    res = run_bass_kernel_spmd(nc, in_maps, core_ids=list(range(8)))
    out = np.empty((2, 8192, D), np.float32)
    for c in range(8):
        o = res.results[c]["out_fm"]
        out[0, 1024 * c:1024 * (c + 1)] = o[:, 0:1024].T
        out[1, 1024 * (7 - c):1024 * (8 - c)] = o[:, 1024:2048].T
    return out


# version 22
# speedup vs baseline: 1.1242x; 1.1242x over previous
import numpy as np
import ml_dtypes
from contextlib import ExitStack

import concourse.bass as bass
import concourse.bacc as bacc
import concourse.mybir as mybir
from concourse.bass_utils import run_bass_kernel_spmd

F32 = mybir.dt.float32
BF16 = mybir.dt.bfloat16
I32 = mybir.dt.int32
AF = mybir.ActivationFunctionType
OP = mybir.AluOpType

D = 4096
NCH = 32
TSS = 256
PRE = 9216
NSS = PRE // TSS
KC = 1024
NSLOT = PRE // KC
NPASS = 4
OWN = 2048
NOSS = OWN // TSS
EPS = 1e-5
BF = ml_dtypes.bfloat16

ENGS = ("sync", "tensor", "vector", "scalar", "gpsimd")


class Sem:
    def __init__(self, nc, es, name):
        self.h = es.enter_context(nc.semaphore(name))
        self.n = 0

    def inc(self, step=1):
        self.n += step
        return self.n


class Plan:
    def __init__(self):
        self.q = {e: [] for e in ENGS}
        self.last_wait = {e: {} for e in ENGS}
        self.state = {}
        self.ev = {e: [] for e in ENGS}

    def emit(self, eng, fn):
        self.q[eng].append(fn)

    def check(self):
        pos = {e: 0 for e in ENGS}
        cnt = {}
        progress = True
        while progress:
            progress = False
            for e in ENGS:
                while pos[e] < len(self.ev[e]):
                    kind, s, v = self.ev[e][pos[e]]
                    if kind == 'w' and cnt.get(id(s), 0) < v:
                        break
                    if kind == 'i':
                        cnt[id(s)] = cnt.get(id(s), 0) + v
                    pos[e] += 1
                    progress = True
        stuck = {e: (pos[e], len(self.ev[e])) for e in ENGS
                 if pos[e] < len(self.ev[e])}
        if stuck:
            detail = {}
            for e in stuck:
                kind, s, v = self.ev[e][pos[e]]
                detail[e] = (pos[e], kind, getattr(s, 'name', '?'), v,
                             cnt.get(id(s), 0))
            raise RuntimeError(f"plan deadlock: {detail}")

    def wait(self, eng, sem, val=None):
        v = sem.n if val is None else val
        assert v <= sem.n, f"wait {v} > planned {sem.n} ({eng})"
        if v <= 0:
            return
        lw = self.last_wait[eng]
        if lw.get(id(sem), -1) >= v:
            return
        lw[id(sem)] = v
        h = sem.h
        self.ev[eng].append(('w', sem, v))
        self.emit(eng, lambda e, h=h, v=v: e.wait_ge(h, v))

    def waitm(self, eng, m):
        if m is not None:
            self.wait(eng, m[0], m[1])

    def barrier(self, sems):
        for e in ENGS:
            for s in sems:
                self.wait(e, s)

    def play(self, nc):
        with nc.Block() as blk:
            @blk.sync
            def _(eng):
                for f in self.q["sync"]:
                    f(eng)

            @blk.tensor
            def _(eng):
                for f in self.q["tensor"]:
                    f(eng)

            @blk.vector
            def _(eng):
                for f in self.q["vector"]:
                    f(eng)

            @blk.scalar
            def _(eng):
                for f in self.q["scalar"]:
                    f(eng)

            @blk.gpsimd
            def _(eng):
                for f in self.q["gpsimd"]:
                    f(eng)


class NS:
    pass


def build_program(debug_spills=False):
    nc = bacc.Bacc("TRN2", target_bir_lowering=False)
    es = ExitStack()

    def din(name, shape, dt):
        return nc.dram_tensor(name, shape, dt, kind="ExternalInput").ap()

    x_fm = din("x_fm", [128, NSS * NCH * TSS], F32)
    x_own = din("x_own", [128, NOSS * NCH * TSS], F32)
    A_fmd = din("A_fm", [128, NCH * 128], F32)
    Bw_fmd = din("Bw_fm", [128, D], F32)
    route_fmd = din("route_fm", [128, NCH * 8], F32)
    bv_qd = din("bv_q", [4, 1024], F32)
    bv_kd = din("bv_k", [4, 1024], F32)
    bv_vd = din("bv_v", [4, 1024], F32)
    identf_d = din("identf", [128, 128], F32)
    mask_tri_d = din("mask_tri", [128, 16 * 512], BF16)
    mask_ones_d = din("mask_ones", [128, 16 * 512], BF16)
    Eb_d = din("E_b", [8, 128], BF16)
    tbl_d = din("tbl", [1, 64], I32)

    out_fm = nc.dram_tensor("out_fm", [D, OWN], F32, kind="ExternalOutput").ap()
    spk = "ExternalOutput" if debug_spills else "Internal"
    spill_xh = nc.dram_tensor("spill_xh", [D, PRE], BF16, kind=spk).ap()
    spill_x1own = nc.dram_tensor("spill_x1own", [D, OWN], F32, kind=spk).ap()
    spill_q = nc.dram_tensor("spill_q", [D, OWN], BF16, kind=spk).ap()
    spill_x2 = nc.dram_tensor("spill_x2", [D, OWN], F32, kind=spk).ap()
    var2d = nc.dram_tensor("var2d", [1, PRE], F32, kind=spk).ap()
    rstdd_t = nc.dram_tensor("rstdd", [1, PRE], F32, kind="Internal")
    rstdd = rstdd_t.ap()

    def mk_sb(stack):
        def f(name, shape, dt):
            return stack.enter_context(nc.sbuf_tensor(name, shape, dt))
        return f

    def mk_ps(stack):
        def f(name, shape, dt=F32):
            return stack.enter_context(nc.psum_tensor(name, shape, dt))
        return f

    sb = mk_sb(es)

    Ab = sb("Ab", [128, NCH * 128], BF16)
    Bwb = sb("Bwb", [128, D], BF16)
    rob = sb("rob", [128, NCH * 8], BF16)
    Eb = sb("Eb_s", [8, 128], BF16)
    onescol = sb("onescol", [128, 1], BF16)
    onesrow = sb("onesrow", [1, 128], BF16)
    ones1f = sb("ones1f", [1, 128], F32)
    Asum_b = sb("Asum_b", [1, 128], BF16)
    Rsum_b = sb("Rsum_b", [1, 8], BF16)
    BwSum_b = sb("BwSum_b", [128, 1], BF16)
    qkb_f = sb("qkb_f", [128, 32], F32)
    vb_f = sb("vb_f", [128, 32], F32)
    identf = sb("identf_s", [128, 128], F32)
    rstdcols = sb("rstdcols", [128, 72], F32)
    epscol = sb("epscol", [128, 1], F32)
    tbl_s = sb("tbl_s", [1, 64], I32)

    P = Plan()
    sx = Sem(nc, es, "sx")
    sq2 = Sem(nc, es, "sq2")
    sg = Sem(nc, es, "sg")
    pe = Sem(nc, es, "pe")
    act = Sem(nc, es, "act")
    dve = Sem(nc, es, "dve")
    pool = Sem(nc, es, "pool")
    sld = [Sem(nc, es, "sld0"), Sem(nc, es, "sld1")]
    sld2 = [Sem(nc, es, "sld20"), Sem(nc, es, "sld21")]
    skk = [Sem(nc, es, "skk0"), Sem(nc, es, "skk1")]
    svv = [Sem(nc, es, "svv0"), Sem(nc, es, "svv1")]
    sv2 = [Sem(nc, es, "sv20"), Sem(nc, es, "sv21")]
    ssp = [Sem(nc, es, "ssp0"), Sem(nc, es, "ssp1")]
    swr = Sem(nc, es, "swr")
    smk = Sem(nc, es, "smk")
    ALLSEMS = (sx, sq2, sg, pe, act, dve, pool,
               sld[0], sld[1], sld2[0], sld2[1], skk[0], skk[1],
               svv[0], svv[1], sv2[0], sv2[1], ssp[0], ssp[1], swr, smk)
    QSEM = {"sync": sx, "scalar": sq2, "gpsimd": sg}
    ESEM = {"tensor": pe, "scalar": act, "vector": dve, "gpsimd": pool}

    def dma(eng, out, in_, sem=None, **kw):
        s = QSEM[eng] if sem is None else sem
        v = s.inc(16)
        P.ev[eng].append(('i', s, 16))
        def f(e, out=out, in_=in_, kw=kw, h=s.h):
            e.dma_start(out=out, in_=in_, **kw).then_inc(h, 16)
        P.emit(eng, f)
        return v

    def op(eng, meth, *args, **kw):
        s = ESEM[eng]
        v = s.inc(1)
        P.ev[eng].append(('i', s, 1))
        def f(e, meth=meth, args=args, kw=kw, h=s.h):
            getattr(e, meth)(*args, **kw).then_inc(h, 1)
        P.emit(eng, f)
        return v

    def opl(eng, last, meth, *args, **kw):
        return (op if last else opn)(eng, meth, *args, **kw)

    def opn(eng, meth, *args, **kw):
        def f(e, meth=meth, args=args, kw=kw):
            getattr(e, meth)(*args, **kw)
        P.emit(eng, f)

    def fm_dram(ap):
        return ap.rearrange("(c p) t -> p c t", p=128)

    es_i = ExitStack()
    sbi = mk_sb(es_i)
    psi = mk_ps(es_i)

    dma("sync", tbl_s[:], tbl_d)
    dma("sync", identf[:], identf_d)
    dma("sync", Eb[:], Eb_d)
    dma("gpsimd", Ab[:], A_fmd)
    dma("gpsimd", Bwb[:], Bw_fmd)
    dma("gpsimd", rob[:], route_fmd)
    bvs = {k: sbi(f"bv{k}_s", [4, 1024], F32) for k in "qkv"}
    dma("sync", bvs["q"][:], bv_qd)
    dma("sync", bvs["k"][:], bv_kd)
    v_ld_bv = dma("sync", bvs["v"][:], bv_vd)

    opn("gpsimd", "memset", onescol[:], 1.0)
    opn("gpsimd", "memset", onesrow[:], 1.0)
    opn("gpsimd", "memset", ones1f[:], 1.0)
    v_rob = op("gpsimd", "memset", epscol[:], EPS)
    nc.const_aps.register(F32, EPS, epscol[0:1, 0:1])

    P.wait("scalar", sx, v_ld_bv)
    babs = {k: sbi(f"babs_{k}", [4, 1024], F32) for k in "qkv"}
    sgn = {k: sbi(f"sgn_{k}", [4, 1024], F32) for k in "qkv"}
    for k in "qkv":
        opn("scalar", "activation", babs[k][:], bvs[k][:], AF.Abs)
    for k in "qk":
        opn("scalar", "sign", sgn[k][:], bvs[k][:])
    v_bv = op("scalar", "sign", sgn["v"][:], bvs["v"][:])

    alph = {}
    tscr = sbi("tscr", [4, 512], F32)
    P.wait("vector", act, v_bv)
    P.wait("gpsimd", act, v_bv)
    for k in "qkv":
        a = sbi(f"alpha_{k}", [4, 1], F32)
        alph[k] = a
        width, step = 1024, 0
        cur = babs[k]
        while width > 1:
            half = width // 2
            eng = ("vector", "gpsimd")[step % 2]
            dst = tscr[:, 0:half] if half > 1 else a[:]
            vv = op(eng, "tensor_add", dst, cur[:, 0:half], cur[:, half:width])
            oth = ("vector", "gpsimd")[(step + 1) % 2]
            P.wait(oth, ESEM[eng], vv)
            cur = tscr
            width, step = half, step + 1
        P.wait("vector", ESEM[("vector", "gpsimd")[(step - 1) % 2]], vv)
        P.wait("gpsimd", ESEM[("vector", "gpsimd")[(step - 1) % 2]], vv)

    sqs = sbi("sqs", [4, 1024], F32)
    for k in "qkv":
        v1 = op("gpsimd", "tensor_mul", sqs[:], sgn[k][:], sgn[k][:])
        P.wait("vector", pool, v1)
        v1 = op("vector", "tensor_sub", sgn[k][:], sgn[k][:], sqs[:])
        P.wait("gpsimd", dve, v1)
        v1 = op("gpsimd", "tensor_scalar_add", sgn[k][:], sgn[k][:], 1.0)
        P.wait("vector", pool, v1)

    qkb_flat = sbi("qkb_flat", [4, 1024], F32)
    vb_flat = sbi("vb_flat", [4, 1024], F32)
    aqk = sbi("aqk", [4, 1], F32)
    v1 = op("vector", "tensor_mul", qkb_flat[:], sgn["q"][:], sgn["k"][:])
    v2 = op("vector", "tensor_mul", aqk[:], alph["q"][:], alph["k"][:])
    P.wait("gpsimd", dve, v2)
    v2 = op("gpsimd", "tensor_scalar_mul", aqk[:], aqk[:], 0.125 / (1024.0 * 1024.0))
    P.wait("vector", pool, v2)
    v_qkb = op("vector", "tensor_single_scalar", qkb_flat[:], qkb_flat[:],
               aqk[:], OP.mult)
    v2 = op("vector", "tensor_scalar_mul", alph["v"][:], alph["v"][:], 1.0 / 1024.0)
    P.wait("gpsimd", dve, v2)
    v_vbf = op("gpsimd", "tensor_single_scalar", vb_flat[:], sgn["v"][:],
               alph["v"][:], OP.mult)

    ps_i = psi("ps_i", [128, 512], F32)
    ps_r = psi("ps_r", [1, 512], F32)
    P.wait("tensor", dve, v_qkb)
    P.wait("tensor", pool, v_vbf)
    P.wait("tensor", sx)
    for cc in range(8):
        opn("tensor", "transpose", ps_i[:, 4 * cc:4 * cc + 4],
            qkb_flat[:, 128 * cc:128 * (cc + 1)], identf[0:4, 0:4])
    for cc in range(8):
        v_pe_t = opl("tensor", cc == 7, "transpose",
                     ps_i[:, 32 + 4 * cc:32 + 4 * cc + 4],
                     vb_flat[:, 128 * cc:128 * (cc + 1)], identf[0:4, 0:4])
    P.wait("scalar", pe, v_pe_t)
    for cc in range(8):
        opn("scalar", "copy", bass.AP(qkb_f, cc, [[32, 128], [8, 4]]),
            ps_i[:, 4 * cc:4 * cc + 4])
        v_qv_done = opl("scalar", cc == 7, "copy",
                        bass.AP(vb_f, cc, [[32, 128], [8, 4]]),
                        ps_i[:, 32 + 4 * cc:32 + 4 * cc + 4])

    P.wait("tensor", sg)
    P.wait("tensor", pool, v_rob)
    for c in range(NCH):
        v = opl("tensor", c == NCH - 1, "matmul", ps_r[0:1, 0:128], onescol[:],
                Ab[:, 128 * c:128 * (c + 1)], start=(c == 0), stop=(c == NCH - 1))
    P.wait("scalar", pe, v)
    v_as = op("scalar", "copy", Asum_b[:], ps_r[0:1, 0:128])
    P.wait("tensor", act, v_as)
    for c in range(NCH):
        v = opl("tensor", c == NCH - 1, "matmul", ps_r[0:1, 128:136], onescol[:],
                rob[:, 8 * c:8 * c + 8], start=(c == 0), stop=(c == NCH - 1))
    P.wait("scalar", pe, v)
    v_rs = op("scalar", "copy", Rsum_b[:], ps_r[0:1, 128:136])

    bws = sbi("bws", [128, 2048], F32)
    P.wait("vector", sg)
    vv = op("vector", "tensor_add", bws[:, 0:2048], Bwb[:, 0:2048], Bwb[:, 2048:4096])
    width, step = 2048, 1
    while width > 1:
        half = width // 2
        eng = ("vector", "gpsimd")[step % 2]
        P.wait(eng, ESEM[("vector", "gpsimd")[(step - 1) % 2]], vv)
        vv = op(eng, "tensor_add", bws[:, 0:half], bws[:, 0:half], bws[:, half:width])
        width, step = half, step + 1
    P.wait("scalar", ESEM[("vector", "gpsimd")[(step - 1) % 2]], vv)
    v_bws = op("scalar", "copy", BwSum_b[:], bws[:, 0:1])

    P.barrier(ALLSEMS)
    es_i.close()

    def alloc_prod(stack, with_xh, pfx=""):
        s0 = mk_sb(stack)
        p0 = mk_ps(stack)
        s = lambda n, sh, dt: s0(pfx + n, sh, dt)
        p = lambda n, sh, dt=F32: p0(pfx + n, sh, dt)
        B = NS()
        B.xf = [s("xf0", [128, NCH * TSS], F32), s("xf1", [128, NCH * TSS], F32)]
        B.xb = s("xb", [128, NCH * TSS], BF16)
        B.sqx = s("sqx", [128, NCH * TSS], BF16)
        B.x1f = s("x1f", [128, NCH * TSS], F32)
        B.xh = ([s("xh0", [128, NCH * TSS], BF16),
                 s("xh1", [128, NCH * TSS], BF16)] if with_xh else None)
        B.murow2 = s("murow2", [1, 2 * TSS], F32)
        B.murow_nb = s("murow_nb", [1, 2 * TSS], BF16)
        B.s2row = s("s2row", [1, 2 * TSS], F32)
        B.sfrow = s("sfrow", [1, 2 * TSS], F32)
        B.lsb = s("lsb", [8, 2 * TSS], F32)
        B.hb = s("hb_t", [128, 2 * TSS], BF16)
        B.musq = s("musq", [1, TSS], F32)
        B.var1 = s("var1", [1, TSS], F32)
        B.r1sq_row = s("r1sq_row", [1, TSS], F32)
        B.r1sq_col = s("r1sq_col", [128, 4], F32)
        B.m1sq = s("m1sq", [1, TSS], F32)
        B.ltok = s("ltok", [128, 16], F32)
        B.m4t = s("m4t", [128, 8], F32)
        B.m2t = s("m2t", [128, 4], F32)
        B.m1t = s("m1t", [128, 2], F32)
        B.difft = s("difft", [128, 16], F32)
        B.sgnt = s("sgnt", [128, 16], F32)
        B.rwt = s("rwt", [128, 32], F32)
        B.rw2t = s("rw2t", [128, 16], F32)
        B.rwT_b = s("rwT_b", [8, TSS], BF16)
        B.wh = s("wh_t", [128, TSS], BF16)
        B.ps_sx = p("ps_sx", [1, 512], F32)
        B.ps_sq = p("ps_sq", [1, 512], F32)
        B.ps_log = p("ps_log", [128, 512], F32)
        B.ps_h = p("ps_h", [128, 512], F32)
        B.ps_f = [p("ps_f0", [128, 512], F32), p("ps_f1", [128, 512], F32),
                  p("ps_f2", [128, 512], F32)]
        B.ps_rep = p("ps_rep", [128, 512], F32)
        B.M = {
            "xf_dve": [None, None], "xf_act": [None, None],
            "xb_rd": None, "sqx_rd": None,
            "xh_dma": [None, None], "v2row_dma": [None, None],
            "pssx_rd": None, "pssq_rd": None, "pslog_rd": None, "psh_rd": None,
            "psf_rd": [None, None, None], "pst_rd": None, "psrep_rd": None,
            "pslog_rd2": None, "psrep_act": None,
            "wh_rd": None, "sq1_rd": None, "x1f_dma": None, "cl_prev": None,
            "rw2t_pe": None, "s2w_dve": None, "s2row_rd": None, "m1sq_rd": None,
        }
        return B

    def stage_a(B, i, src, mode):
        M = B.M
        q = i % 2
        pc = q * TSS
        R0 = TSS * i
        xfq, xbq, sqq = B.xf[q], B.xb, B.sqx
        P.waitm("sync", M["xf_dve"][q])
        P.waitm("sync", M["xf_act"][q])
        if mode == "sub3":
            for g in range(2):
                v_ld = dma("sync",
                           xfq.rearrange("p (c t) -> p c t", t=TSS)[:, 8 * g:8 * g + 8, :],
                           fm_dram(src)[:, 8 * g:8 * g + 8, R0:R0 + TSS],
                           sem=sld[q])
            P.waitm("scalar", M["xf_dve"][q])
            P.waitm("scalar", M["xf_act"][q])
            for g in range(2, 4):
                v_ld2 = dma("scalar",
                            xfq.rearrange("p (c t) -> p c t", t=TSS)[:, 8 * g:8 * g + 8, :],
                            fm_dram(src)[:, 8 * g:8 * g + 8, R0:R0 + TSS],
                            sem=sld2[q])
        else:
            half = NCH * TSS // 2
            v_ld = dma("sync", xfq[:, 0:half],
                       src[:, NCH * TSS * i:NCH * TSS * i + half], sem=sld[q])
            P.waitm("scalar", M["xf_dve"][q])
            P.waitm("scalar", M["xf_act"][q])
            v_ld2 = dma("scalar", xfq[:, half:2 * half],
                        src[:, NCH * TSS * i + half:NCH * TSS * (i + 1)],
                        sem=sld2[q])
        P.waitm("gpsimd", M["xb_rd"])
        if mode == "sub3":
            for g in range(4):
                v_xb = dma("gpsimd",
                           xbq.rearrange("p (c t) -> p c t", t=TSS)[:, 8 * g:8 * g + 8, :],
                           fm_dram(src)[:, 8 * g:8 * g + 8, R0:R0 + TSS])
        else:
            v_xb = dma("gpsimd", xbq[:], src[:, NCH * TSS * i:NCH * TSS * (i + 1)])
        P.wait("scalar", sld[q], v_ld)
        P.wait("scalar", sld2[q], v_ld2)
        P.waitm("scalar", M["sqx_rd"])
        P.waitm("scalar", M["sq1_rd"])
        v_sq = op("scalar", "activation", sqq[:], xfq[:], AF.Square)
        P.wait("tensor", sg, v_xb)
        P.waitm("tensor", M["pssx_rd"])
        for c in range(NCH):
            v_g0 = opl("tensor", c == NCH - 1, "matmul", B.ps_sx[0:1, 0:TSS],
                       onescol[:], xbq[:, TSS * c:TSS * (c + 1)],
                       start=(c == 0), stop=(c == NCH - 1))
        P.wait("tensor", act, v_sq)
        P.waitm("tensor", M["pssq_rd"])
        for c in range(NCH):
            v_g2 = opl("tensor", c == NCH - 1, "matmul", B.ps_sq[0:1, 0:TSS],
                       onescol[:], sqq[:, TSS * c:TSS * (c + 1)],
                       start=(c == 0), stop=(c == NCH - 1))
        P.wait("scalar", pe, v_g0)
        v_mu = op("scalar", "activation", B.murow2[0:1, pc:pc + TSS],
                  B.ps_sx[0:1, 0:TSS], AF.Copy, scale=-1.0 / D)
        v_mub = op("scalar", "activation", B.murow_nb[0:1, pc:pc + TSS],
                   B.ps_sx[0:1, 0:TSS], AF.Copy, scale=-1.0 / D)
        M["pssx_rd"] = (act, v_mub)
        P.waitm("tensor", M["pslog_rd"])
        for c in range(NCH):
            opn("tensor", "matmul", B.ps_log[0:8, 0:TSS], rob[:, 8 * c:8 * (c + 1)],
                xbq[:, TSS * c:TSS * (c + 1)], start=(c == 0), stop=False)
        P.wait("tensor", act, v_mub)
        v_g1c = op("tensor", "matmul", B.ps_log[0:8, 0:TSS], Rsum_b[:],
                   B.murow_nb[0:1, pc:pc + TSS], start=False, stop=True)
        P.waitm("tensor", M["psh_rd"])
        for c in range(NCH):
            v_g3m = opl("tensor", c == NCH - 1, "matmul", B.ps_h[:, 0:TSS],
                        Ab[:, 128 * c:128 * (c + 1)],
                        xbq[:, TSS * c:TSS * (c + 1)], start=(c == 0), stop=False)
        v_g3c = op("tensor", "matmul", B.ps_h[:, 0:TSS], Asum_b[:],
                   B.murow_nb[0:1, pc:pc + TSS], start=False, stop=True)
        M["xb_rd"] = (pe, v_g3c)
        M["sqx_rd"] = (pe, v_g2)
        P.wait("scalar", pe, v_g2)
        P.waitm("scalar", M["s2w_dve"])
        v_s2 = op("scalar", "activation", B.s2row[0:1, pc:pc + TSS],
                  B.ps_sq[0:1, 0:TSS], AF.Copy, scale=1.0 / D, bias=EPS)
        M["pssq_rd"] = (act, v_s2)
        P.wait("scalar", pe, v_g1c)
        v_lsb = op("scalar", "copy", B.lsb[0:8, pc:pc + TSS], B.ps_log[0:8, 0:TSS])
        return dict(i=i, q=q, pc=pc, R0=R0, v_sq=v_sq, v_s2=v_s2, v_mu=v_mu,
                    v_lsb=v_lsb, v_g3c=v_g3c)

    def a2_head(B, ctx):
        M = B.M
        pc = ctx["pc"]
        v_musq = op("scalar", "activation", B.musq[:], B.murow2[0:1, pc:pc + TSS],
                    AF.Square)
        P.wait("gpsimd", act, v_musq)
        P.wait("gpsimd", act, ctx["v_s2"])
        P.waitm("gpsimd", M["v2row_dma"][0])
        vv = op("gpsimd", "tensor_sub", B.var1[:], B.s2row[0:1, pc:pc + TSS],
                B.musq[:])
        M["s2row_rd"] = (pool, vv)
        P.wait("vector", pool, vv)
        ctx["v_r1"] = op("vector", "reciprocal", B.r1sq_row[:], B.var1[:])

    def a2_mid(B, ctx):
        M = B.M
        q = ctx["q"]
        P.wait("tensor", dve, ctx["v_r1"])
        P.waitm("tensor", M["pslog_rd"])
        P.waitm("tensor", M["pslog_rd2"])
        for tt in range(2):
            v_tr = opl("tensor", tt == 1, "transpose", B.ps_log[:, 256 + tt:257 + tt],
                       B.r1sq_row[0:1, 128 * tt:128 * (tt + 1)], identf[0:1, 0:1])
        P.wait("scalar", pe, v_tr)
        v_r1c = op("scalar", "copy", B.r1sq_col[:, 2 * q:2 * q + 2],
                   B.ps_log[:, 256:258])
        M["pslog_rd"] = (act, v_r1c)
        ctx["v_r1c"] = v_r1c

    def a2_tail(B, ctx):
        M = B.M
        i, q, pc = ctx["i"], ctx["q"], ctx["pc"]
        P.wait("tensor", act, ctx["v_lsb"])
        P.wait("tensor", act, ctx["v_r1c"])
        for tt in range(2):
            v_trl = opl("tensor", tt == 1, "transpose",
                        B.ps_log[:, 264 + 8 * tt:272 + 8 * tt],
                        B.lsb[0:8, pc + 128 * tt:pc + 128 * (tt + 1)],
                        identf[0:8, 0:8])
        P.wait("scalar", pe, ctx["v_g3c"])
        v_hb = op("scalar", "copy", B.hb[:, pc:pc + TSS], B.ps_h[:, 0:TSS])
        M["psh_rd"] = (act, v_hb)
        P.wait("vector", pe, v_trl)
        P.waitm("vector", M["cl_prev"])
        v_lt = op("vector", "tensor_copy", B.ltok[:], B.ps_log[:, 264:280])
        M["pslog_rd2"] = (dve, v_lt)
        P.wait("vector", dve, v_lt)
        opn("vector", "tensor_max", B.m4t[:, 0:4], B.ltok[:, 0:4], B.ltok[:, 4:8])
        v4 = op("vector", "tensor_max", B.m4t[:, 4:8], B.ltok[:, 8:12], B.ltok[:, 12:16])
        P.wait("vector", dve, v4)
        opn("vector", "tensor_max", B.m2t[:, 0:2], B.m4t[:, 0:2], B.m4t[:, 2:4])
        v2 = op("vector", "tensor_max", B.m2t[:, 2:4], B.m4t[:, 4:6], B.m4t[:, 6:8])
        P.wait("vector", dve, v2)
        opn("vector", "tensor_max", B.m1t[:, 0:1], B.m2t[:, 0:1], B.m2t[:, 1:2])
        vv = op("vector", "tensor_max", B.m1t[:, 1:2], B.m2t[:, 2:3], B.m2t[:, 3:4])
        P.wait("gpsimd", dve, vv)
        opn("gpsimd", "tensor_single_scalar", B.difft[:, 0:8], B.ltok[:, 0:8],
            B.m1t[:, 0:1], OP.subtract)
        vv = op("gpsimd", "tensor_single_scalar", B.difft[:, 8:16], B.ltok[:, 8:16],
                B.m1t[:, 1:2], OP.subtract)
        P.wait("scalar", pool, vv)
        v_sg2 = op("scalar", "sign", B.sgnt[:], B.difft[:])
        P.wait("vector", act, v_sg2)
        v_rw = op("vector", "scalar_tensor_tensor", B.rwt[:, 16 * q:16 * q + 16],
                  B.sgnt[:], 1.0, B.ltok[:], op0=OP.add, op1=OP.mult)
        M["cl_prev"] = (dve, v_rw)
        ctx["v_rw"] = v_rw


    def b_head(B, ctx):
        M = B.M
        q = ctx["q"]
        P.wait("gpsimd", dve, ctx["v_rw"])
        P.wait("gpsimd", act, ctx["v_r1c"])
        P.waitm("gpsimd", M["rw2t_pe"])
        opn("gpsimd", "tensor_single_scalar", B.rw2t[:, 0:8],
            B.rwt[:, 16 * q:16 * q + 8], B.r1sq_col[:, 2 * q:2 * q + 1], OP.mult)
        v_rw2 = op("gpsimd", "tensor_single_scalar", B.rw2t[:, 8:16],
                   B.rwt[:, 16 * q + 8:16 * q + 16],
                   B.r1sq_col[:, 2 * q + 1:2 * q + 2], OP.mult)
        P.wait("tensor", pool, v_rw2)
        P.waitm("tensor", M["psrep_rd"])
        P.waitm("tensor", M["psrep_act"])
        for tt in range(2):
            v_trw = opl("tensor", tt == 1, "transpose",
                        B.ps_rep[0:8, 128 + 128 * tt:256 + 128 * tt],
                        B.rw2t[:, 8 * tt:8 * (tt + 1)], identf[:, :])
        P.wait("scalar", pe, v_trw)
        M["rw2t_pe"] = (pe, v_trw)
        v_rwT = op("scalar", "copy", B.rwT_b[0:8, 0:TSS],
                   B.ps_rep[0:8, 128:128 + TSS])
        M["psrep_act"] = (act, v_rwT)
        ctx["v_rwT"] = v_rwT

    def b_mid(B, ctx):
        M = B.M
        pc = ctx["pc"]
        P.wait("tensor", act, ctx["v_rwT"])
        v_rwe = op("tensor", "matmul", B.ps_rep[:, 0:TSS], Eb[:],
                   B.rwT_b[0:8, 0:TSS], start=True, stop=True)
        P.wait("vector", pe, v_rwe)
        P.waitm("vector", M["wh_rd"])
        v_wh = op("vector", "tensor_tensor", B.wh[:], B.hb[:, pc:pc + TSS],
                  B.ps_rep[:, 0:TSS], op=OP.mult)
        ctx["v_wh"] = v_wh

    def b_bulk(B, ctx):
        M = B.M
        q = ctx["q"]
        xfq = B.xf[q]
        P.wait("tensor", dve, ctx["v_wh"])
        ev = [M["psf_rd"][0], M["psf_rd"][1], M["psf_rd"][2]]
        for k in range(16):
            par = k % 3
            P.waitm("tensor", ev[par])
            for half in range(2):
                oc = 2 * k + half
                v_fp = opl("tensor", half == 1, "matmul",
                           B.ps_f[par][:, TSS * half:TSS * (half + 1)],
                           Bwb[:, 128 * oc:128 * (oc + 1)], B.wh[:],
                           start=True, stop=True)
            P.wait("vector", pe, v_fp)
            if k == 0:
                P.waitm("vector", M["x1f_dma"])
            sl = slice(2 * k * TSS, (2 * k + 2) * TSS)
            v_ev = op("vector", "scalar_tensor_tensor", B.x1f[:, sl],
                      B.ps_f[par][:, 0:2 * TSS], 0.5, xfq[:, sl],
                      op0=OP.mult, op1=OP.add)
            ev[par] = (dve, v_ev)
        M["psf_rd"] = ev
        ctx["last_g4"] = v_fp
        ctx["v_evall"] = max(x[1] for x in ev)

    def b_tail(B, ctx, src, mode):
        M = B.M
        i, q, pc, R0 = ctx["i"], ctx["q"], ctx["pc"], ctx["R0"]
        xfq = B.xf[q]
        v_wh, last_g4, v_evall = ctx["v_wh"], ctx["last_g4"], ctx["v_evall"]
        if mode == "sub3":
            P.wait("sync", dve, v_evall)
            for g in range(4):
                v_out = dma("sync",
                            fm_dram(out_fm)[:, 8 * g:8 * g + 8, R0:R0 + TSS],
                            B.x1f.rearrange("p (c t) -> p c t", t=TSS)[:, 8 * g:8 * g + 8, :],
                            sem=swr)
            M["x1f_dma"] = (swr, v_out)
            M["wh_rd"] = (pe, last_g4)
            M["psrep_rd"] = (dve, v_wh)
            M["xf_dve"][q] = (dve, v_evall)
            M["xf_act"][q] = (act, ctx["v_sq"])
            return v_out
        v_g5 = op("tensor", "matmul", B.ps_rep[0:1, 256:256 + TSS], BwSum_b[:],
                  B.wh[:], start=True, stop=True)
        M["wh_rd"] = (pe, v_g5)
        P.wait("scalar", pe, v_g5)
        v_sf = op("scalar", "activation", B.sfrow[0:1, pc:pc + TSS],
                  B.ps_rep[0:1, 256:256 + TSS], AF.Copy, scale=-0.5 / D)
        M["psrep_act"] = (act, v_sf)
        P.wait("tensor", dve, v_wh)
        P.wait("tensor", act, v_sf)
        opn("tensor", "matmul", B.ps_rep[:, 0:TSS], ones1f[:],
            B.murow2[0:1, pc:pc + TSS], start=True, stop=False)
        v_g7 = op("tensor", "matmul", B.ps_rep[:, 0:TSS], ones1f[:],
                  B.sfrow[0:1, pc:pc + TSS], start=False, stop=True)
        P.wait("vector", pe, v_g7)
        rep_bc = B.ps_rep[:, 0:TSS].unsqueeze(1).broadcast_to([128, NCH, TSS])
        if mode == "pre":
            xhq = B.xh[q]
            P.waitm("vector", M["xh_dma"][q])
            P.wait("vector", dve, v_evall)
            v_ctr = op("vector", "tensor_tensor",
                       xhq.rearrange("p (c t) -> p c t", t=TSS),
                       B.x1f.rearrange("p (c t) -> p c t", t=TSS), rep_bc,
                       op=OP.add)
            M["psrep_rd"] = (dve, v_ctr)
            P.wait("sync", dve, v_ctr)
            for g in range(4):
                v_sp = dma("sync",
                           fm_dram(spill_xh)[:, 8 * g:8 * g + 8, R0:R0 + TSS],
                           xhq.rearrange("p (c t) -> p c t", t=TSS)[:, 8 * g:8 * g + 8, :],
                           sem=ssp[q])
            M["xh_dma"][q] = (ssp[q], v_sp)
            M["xf_dve"][q] = (dve, v_evall)
            M["xf_act"][q] = (act, ctx["v_sq"])
            M["psrep_rd"] = (dve, v_ctr)
            ctx["v_evall"] = v_evall
            ctx["v_ctr"] = v_ctr
            return v_sp
        P.wait("sync", dve, v_evall)
        for g in range(4):
            v_spx1 = dma("sync",
                         fm_dram(spill_x1own)[:, 8 * g:8 * g + 8, R0:R0 + TSS],
                         B.x1f.rearrange("p (c t) -> p c t", t=TSS)[:, 8 * g:8 * g + 8, :],
                         sem=swr)
        P.wait("vector", dve, v_evall)
        v_ctr = op("vector", "tensor_tensor",
                   xfq.rearrange("p (c t) -> p c t", t=TSS),
                   B.x1f.rearrange("p (c t) -> p c t", t=TSS), rep_bc, op=OP.add)
        M["psrep_rd"] = (dve, v_ctr)
        P.wait("vector", swr, v_spx1)
        P.wait("vector", dve, v_ctr)
        qkb_bc = qkb_f[:].unsqueeze(2).broadcast_to([128, NCH, TSS])
        v_qt = op("vector", "tensor_tensor",
                  B.x1f.rearrange("p (c t) -> p c t", t=TSS),
                  xfq.rearrange("p (c t) -> p c t", t=TSS), qkb_bc, op=OP.mult)
        P.waitm("vector", M["xh_dma"][q])
        P.wait("vector", dve, v_qt)
        rep_own = rstdownrep[:, R0:R0 + TSS].unsqueeze(1).broadcast_to(
            [128, NCH, TSS])
        v_qb = op("vector", "tensor_tensor",
                  B.xh[q].rearrange("p (c t) -> p c t", t=TSS),
                  B.x1f.rearrange("p (c t) -> p c t", t=TSS), rep_own, op=OP.mult)
        P.wait("sync", dve, v_qb)
        for g in range(4):
            v_spq = dma("sync",
                        fm_dram(spill_q)[:, 8 * g:8 * g + 8, R0:R0 + TSS],
                        B.xh[q].rearrange("p (c t) -> p c t", t=TSS)[:, 8 * g:8 * g + 8, :],
                        sem=ssp[q])
        M["xh_dma"][q] = (ssp[q], v_spq)
        M["xf_dve"][q] = (dve, v_qt)
        M["xf_act"][q] = (act, ctx["v_sq"])
        M["x1f_dma"] = (swr, v_spx1)
        return v_spq

    def bdef_sq(B, ctx):
        M = B.M
        P.wait("scalar", dve, ctx["v_evall"])
        P.waitm("scalar", M["sqx_rd"])
        ctx["v_sq1"] = op("scalar", "activation", B.sqx[:], B.x1f[:], AF.Square)

    def bdef_rest(B, ctx):
        M = B.M
        q, pc, R0 = ctx["q"], ctx["pc"], ctx["R0"]
        ev = M["psf_rd"]
        P.wait("tensor", act, ctx["v_sq1"])
        P.waitm("tensor", ev[0])
        for c in range(NCH):
            v_g6 = opl("tensor", c == NCH - 1, "matmul",
                       B.ps_f[0][0:1, 0:TSS], onescol[:],
                       B.sqx[:, TSS * c:TSS * (c + 1)], start=(c == 0),
                       stop=(c == NCH - 1))
        M["sq1_rd"] = (pe, v_g6)
        P.wait("scalar", pe, v_g6)
        P.waitm("scalar", M["s2row_rd"])
        v_s2b = op("scalar", "activation", B.s2row[0:1, pc:pc + TSS],
                   B.ps_f[0][0:1, 0:TSS], AF.Copy, scale=1.0 / D)
        M["psf_rd"][0] = (act, v_s2b)
        P.waitm("scalar", M["m1sq_rd"])
        v_m1sq = op("scalar", "activation", B.m1sq[:], B.ps_rep[0:1, 0:TSS],
                    AF.Square)
        M["psrep_act"] = (act, v_m1sq)
        P.wait("vector", act, v_m1sq)
        P.wait("vector", act, v_s2b)
        P.waitm("vector", M["v2row_dma"][0])
        v_v2 = op("vector", "tensor_sub", B.var1[:],
                  B.s2row[0:1, pc:pc + TSS], B.m1sq[:])
        M["s2w_dve"] = (dve, v_v2)
        M["m1sq_rd"] = (dve, v_v2)
        P.wait("sync", dve, v_v2)
        v_v2d = dma("sync", var2d[0:1, R0:R0 + TSS], B.var1[:], sem=sv2[q])
        M["v2row_dma"][0] = (sv2[q], v_v2d)

    def run_pipeline(B, n, src, mode):
        prev = None
        for i in range(n):
            c = stage_a(B, i, src, mode)
            if prev is not None:
                b_head(B, prev)
            a2_head(B, c)
            if prev is not None:
                b_mid(B, prev)
            a2_mid(B, c)
            a2_tail(B, c)
            if prev is not None:
                b_bulk(B, prev)
                b_tail(B, prev, src, mode)
                if mode == "pre":
                    bdef_sq(B, prev)
                    bdef_rest(B, prev)
            prev = c
        b_head(B, prev)
        b_mid(B, prev)
        b_bulk(B, prev)
        b_tail(B, prev, src, mode)
        if mode == "pre":
            bdef_sq(B, prev)
            bdef_rest(B, prev)

    es_p = ExitStack()
    Bp = alloc_prod(es_p, with_xh=True)
    P.wait("tensor", act, v_as)
    P.wait("tensor", act, v_rs)
    P.wait("tensor", act, v_bws)
    P.wait("tensor", act, v_qv_done)
    run_pipeline(Bp, NSS, x_fm, "pre")

    sbp = mk_sb(es_p)
    v72 = sbp("v72", [72, 128], F32)
    rstdownrow_b = sbp("rstdownrow_b", [1, OWN], BF16)
    rstdownrep = sbp("rstdownrep", [128, OWN], BF16)
    P.wait("sync", sv2[0])
    P.wait("sync", sv2[1])
    v_v72 = dma("sync", v72[:], var2d.rearrange("a (r c) -> (a r) c", c=128))
    P.wait("scalar", sx, v_v72)
    v_ln = op("scalar", "activation", v72[:], v72[:], AF.Ln, bias=epscol[0:72, 0:1])
    P.wait("scalar", act, v_ln)
    v_ex = op("scalar", "activation", v72[:], v72[:], AF.Exp, scale=-0.5)
    P.wait("sync", act, v_ex)
    v_rdd = dma("sync", rstdd.rearrange("a (r c) -> (a r) c", c=128), v72[:])
    P.wait("tensor", act, v_ex)
    P.waitm("tensor", Bp.M["psrep_rd"])
    P.waitm("tensor", Bp.M["psrep_act"])
    v = op("tensor", "transpose", Bp.ps_rep[:, 0:72], v72[:], identf[0:72, 0:72])
    P.wait("scalar", pe, v)
    v_rc = op("scalar", "copy", rstdcols[:], Bp.ps_rep[:, 0:72])
    Bp.M["psrep_act"] = (act, v_rc)

    def f_ownrow(e):
        r = e.alloc_register("rofs")
        e.reg_load(r, tbl_s[0:1, 48:49])
        e.dma_start(out=rstdownrow_b[0:1, 0:1024],
                    in_=bass.AP(rstdd_t, r, [[PRE, 1], [1, 1024]])
                    ).then_inc(sg.h, 16)
    P.wait("gpsimd", sx, v_rdd)
    P.emit("gpsimd", f_ownrow)
    v_orA = sg.inc(16)
    P.ev["gpsimd"].append(('i', sg, 16))
    v_orB = dma("gpsimd", rstdownrow_b[0:1, 1024:2048],
                rstdd[0:1, PRE - 1024:PRE])
    P.ev["gpsimd"].append(('w', sg, v_orB))
    P.emit("gpsimd", lambda e, v=v_orB: e.wait_ge(sg.h, v))
    P.wait("tensor", sg, v_orB)
    P.waitm("tensor", Bp.M["psrep_rd"])
    P.waitm("tensor", Bp.M["psrep_act"])
    v_orr = None
    for seg in range(4):
        if v_orr is not None:
            P.wait("tensor", act, v_orr)
        v = op("tensor", "matmul", Bp.ps_rep[:, 0:512], onesrow[:],
               rstdownrow_b[0:1, 512 * seg:512 * (seg + 1)], start=True, stop=True)
        P.wait("scalar", pe, v)
        v_orr = op("scalar", "copy", rstdownrep[:, 512 * seg:512 * (seg + 1)],
                   Bp.ps_rep[:, 0:512])
    Bp.M["psrep_rd"] = (act, v_orr)

    run_pipeline(Bp, NOSS, x_own, "own")

    P.barrier(ALLSEMS)
    es_p.close()

    es_a = ExitStack()
    sba = mk_sb(es_a)
    psa = mk_ps(es_a)
    Kring = [sba("Kring0", [128, 8 * KC], BF16), sba("Kring1", [128, 8 * KC], BF16)]
    Vring = [sba("Vring0", [128, 8 * KC], BF16), sba("Vring1", [128, 8 * KC], BF16)]
    Qcur = sba("Qcur", [128, 8 * KC], BF16)
    Osb = sba("Osb", [128, 8 * KC], F32)
    Pst = sba("Pst", [128, 16 * 512], BF16)
    sigt = sba("sigt", [128, 4 * 512], BF16)
    maskcur = sba("maskcur", [128, 16 * 512], BF16)
    ps_S = [psa("ps_S0", [128, 1024], F32), psa("ps_S1", [128, 1024], F32)]
    ps_O = [psa("ps_O0", [128, 1024], F32), psa("ps_O1", [128, 1024], F32)]

    spxh = fm_dram(spill_xh)
    spq = fm_dram(spill_q)
    spx1 = fm_dram(spill_x1own)
    spx2 = fm_dram(spill_x2)

    def cond_dma(out, in_, tbl_idx, sem=None, **kw):
        s = sg if sem is None else sem
        v = s.inc(16)
        P.ev["gpsimd"].append(('i', s, 16))
        def f(e, out=out, in_=in_, idx=tbl_idx, kw=kw, h=s.h):
            cv = e.value_load(tbl_s[0:1, idx:idx + 1], min_val=0, max_val=1)
            e.dma_start(out=out, in_=in_, cond=cv, **kw).then_inc(h, 16)
        P.emit("gpsimd", f)
        return v

    AM = {"K_pe": [0, 0], "V_pe": [0, 0], "Q_pe": 0, "mask_pool": 0,
          "O_dve": 0, "sig_rd": [0] * 4, "Pst_rd": 0,
          "psS_rd": [0, 0], "psO_rd": [0, 0], "O_dump": 0}

    for p in range(NPASS):
        P.wait("gpsimd", dve, AM["O_dve"])
        P.wait("gpsimd", sx, AM["O_dump"])
        dma("gpsimd", Osb.rearrange("p (c t) -> p c t", t=KC),
            spx1[:, 8 * p:8 * p + 8, 0:1024])
        P.wait("gpsimd", pe, AM["Q_pe"])
        v_pass = dma("gpsimd", Qcur.rearrange("p (c t) -> p c t", t=KC),
                     spq[:, 8 * p:8 * p + 8, 0:1024])
        P.ev["gpsimd"].append(('w', sg, v_pass))
        P.emit("gpsimd", lambda e, v=v_pass: e.wait_ge(sg.h, v))
        for j in range(NSLOT):
            q = j % 2
            P.wait("gpsimd", dve, AM["O_dve"])
            v_dumpA = cond_dma(spx2[:, 8 * p:8 * p + 8, 0:1024],
                               Osb.rearrange("p (c t) -> p c t", t=KC), j)
            P.ev["gpsimd"].append(('w', sg, v_dumpA))
            P.emit("gpsimd", lambda e, v=v_dumpA: e.wait_ge(sg.h, v))
            v_ob = cond_dma(Osb.rearrange("p (c t) -> p c t", t=KC),
                            spx1[:, 8 * p:8 * p + 8, 1024:2048], j)
            P.wait("gpsimd", pe, AM["Q_pe"])
            v_qb = cond_dma(Qcur.rearrange("p (c t) -> p c t", t=KC),
                            spq[:, 8 * p:8 * p + 8, 1024:2048], j)
            P.wait("gpsimd", dve, AM["mask_pool"])
            cond_dma(maskcur[:], mask_tri_d, 16 + j, sem=smk)
            v_mload = cond_dma(maskcur[:], mask_ones_d, 32 + j, sem=smk)
            P.wait("sync", pe, AM["K_pe"][q])
            v_k = dma("sync", Kring[q].rearrange("p (c t) -> p c t", t=KC),
                      spxh[:, 8 * p:8 * p + 8, KC * j:KC * (j + 1)],
                      sem=skk[q])
            P.wait("scalar", pe, AM["V_pe"][q])
            v_v = None
            for sbk in range(8):
                v_v = dma("scalar", Vring[q][:, KC * sbk:KC * (sbk + 1)],
                          spill_xh[1024 * p:1024 * (p + 1),
                                   KC * j + 128 * sbk:KC * j + 128 * (sbk + 1)],
                          transpose=True, sem=svv[q])
            for sbk in range(8):
                par = sbk % 2
                P.wait("tensor", skk[q], v_k)
                P.wait("tensor", sg, v_qb)
                P.wait("tensor", dve, AM["psS_rd"][par])
                for dc in range(8):
                    for th in range(2):
                        vx = opl("tensor", dc == 7 and th == 1, "matmul",
                                 ps_S[par][:, 512 * th:512 * (th + 1)],
                                 Kring[q][:, KC * dc + 128 * sbk:KC * dc + 128 * (sbk + 1)],
                                 Qcur[:, KC * dc + 512 * th:KC * dc + 512 * (th + 1)],
                                 start=(dc == 0), stop=(dc == 7))
                v_s = vx
                AM["Q_pe"] = v_s
                AM["K_pe"][q] = v_s
                P.wait("scalar", pe, v_s)
                P.wait("vector", smk, v_mload)
                for th in range(2):
                    rot = (2 * sbk + th) % 4
                    P.wait("scalar", dve, AM["sig_rd"][rot])
                    v_sig = op("scalar", "activation",
                               sigt[:, 512 * rot:512 * (rot + 1)],
                               ps_S[par][:, 512 * th:512 * (th + 1)], AF.Sigmoid,
                               scale=rstdcols[:, 8 * j + sbk:8 * j + sbk + 1])
                    P.wait("vector", act, v_sig)
                    P.wait("vector", pe, AM["Pst_rd"])
                    v_pp = op("vector", "scalar_tensor_tensor",
                              Pst[:, 512 * (2 * sbk + th):512 * (2 * sbk + th + 1)],
                              sigt[:, 512 * rot:512 * (rot + 1)],
                              rstdcols[:, 8 * j + sbk:8 * j + sbk + 1],
                              maskcur[:, 512 * (2 * sbk + th):512 * (2 * sbk + th + 1)],
                              op0=OP.mult, op1=OP.mult)
                    AM["sig_rd"][rot] = v_pp
                AM["psS_rd"][par] = v_pp
            AM["mask_pool"] = v_pp
            for oc in range(8):
                par = oc % 2
                P.wait("tensor", dve, v_pp)
                P.wait("tensor", svv[q], v_v)
                P.wait("tensor", dve, AM["psO_rd"][par])
                for sbk in range(8):
                    for th in range(2):
                        vx = opl("tensor", sbk == 7 and th == 1, "matmul",
                                 ps_O[par][:, 512 * th:512 * (th + 1)],
                                 Vring[q][:, KC * sbk + 128 * oc:KC * sbk + 128 * (oc + 1)],
                                 Pst[:, 512 * (2 * sbk + th):512 * (2 * sbk + th + 1)],
                                 start=(sbk == 0), stop=(sbk == 7))
                v_av = vx
                AM["V_pe"][q] = v_av
                AM["Pst_rd"] = v_av
                P.wait("vector", pe, v_av)
                P.wait("vector", sg, v_qb)
                for th in range(2):
                    sl = slice(KC * oc + 512 * th, KC * oc + 512 * (th + 1))
                    v_ev = op("vector", "scalar_tensor_tensor", Osb[:, sl],
                              ps_O[par][:, 512 * th:512 * (th + 1)],
                              vb_f[:, 8 * p + oc:8 * p + oc + 1],
                              Osb[:, sl], op0=OP.mult, op1=OP.add)
                AM["psO_rd"][par] = v_ev
                AM["O_dve"] = v_ev
        P.wait("sync", dve, AM["O_dve"])
        v_dump = dma("sync", spx2[:, 8 * p:8 * p + 8, 1024:2048],
                     Osb.rearrange("p (c t) -> p c t", t=KC))
        AM["O_dump"] = v_dump

    P.barrier(ALLSEMS)
    es_a.close()

    es_s = ExitStack()
    Bs = alloc_prod(es_s, with_xh=False, pfx="s_")
    run_pipeline(Bs, NOSS, spill_x2, "sub3")
    for s_ in ALLSEMS:
        if s_ not in (pe, act, dve, pool):
            P.wait("sync", s_)
    P.wait("gpsimd", swr)

    P.check()
    P.play(nc)
    es_s.close()
    es.close()
    nc.compile()
    return nc


def _stage(inputs):
    x = np.ascontiguousarray(inputs["L0_attn_x"])
    A = inputs["L0_ffn_A"]
    Bw = inputs["L0_ffn_Bw"]
    route = inputs["L0_ffn_route_w"]
    A_flat = A.reshape(128, D)
    A_fm = np.ascontiguousarray(
        A_flat.reshape(128, NCH, 128).transpose(2, 1, 0).reshape(128, NCH * 128))
    Bw_fm = np.ascontiguousarray(Bw.transpose(0, 2, 1).reshape(128, D))
    route_fm = np.ascontiguousarray(
        route.reshape(8, NCH, 128).transpose(2, 1, 0).reshape(128, NCH * 8))
    identf = np.eye(128, dtype=np.float32)
    Emat = np.zeros((8, 128), dtype=BF)
    for e in range(8):
        Emat[e, 16 * e:16 * (e + 1)] = 1
    s_i = np.arange(128)[:, None]
    t_i = np.arange(512)[None, :]
    mtri = np.zeros((128, 16 * 512), dtype=BF)
    for sbk in range(8):
        for th in range(2):
            m = ((sbk * 128 + s_i) <= (th * 512 + t_i)).astype(np.float32)
            mtri[:, 512 * (2 * sbk + th):512 * (2 * sbk + th + 1)] = m.astype(BF)
    mones = np.ones((128, 16 * 512), dtype=BF)

    xT = [np.ascontiguousarray(x[b].T) for b in range(2)]
    in_maps = []
    for c in range(8):
        cA = c + 1
        cB = 8 - c
        xc = np.concatenate([xT[0][:, :1024 * cA], xT[1][:, :1024 * cB]], axis=1)
        xo = np.concatenate([xT[0][:, 1024 * c:1024 * (c + 1)],
                             xT[1][:, 1024 * (7 - c):1024 * (8 - c)]], axis=1)
        xc = xc.reshape(NCH, 128, NSS, TSS).transpose(1, 2, 0, 3).reshape(
            128, NSS * NCH * TSS)
        xo = xo.reshape(NCH, 128, NOSS, TSS).transpose(1, 2, 0, 3).reshape(
            128, NOSS * NCH * TSS)
        tbl = np.zeros((1, 64), np.int32)
        tri_prev = False
        for j in range(NSLOT):
            tbl[0, j] = 1 if j == cA else 0
            tri = (j == cA - 1) or (j == 8)
            tbl[0, 16 + j] = 1 if (tri and not tri_prev) else 0
            tbl[0, 32 + j] = 1 if ((not tri) and (j == 0 or tri_prev)) else 0
            tri_prev = tri
        tbl[0, 48] = (cA - 1) * 1024
        in_maps.append({
            "x_fm": np.ascontiguousarray(xc),
            "x_own": np.ascontiguousarray(xo),
            "A_fm": A_fm, "Bw_fm": Bw_fm, "route_fm": route_fm,
            "bv_q": inputs["L0_attn_bv_q"], "bv_k": inputs["L0_attn_bv_k"],
            "bv_v": inputs["L0_attn_bv_v"],
            "identf": identf, "mask_tri": mtri, "mask_ones": mones,
            "E_b": Emat, "tbl": tbl,
        })
    return in_maps


def kernel(L0_attn_x, L0_attn_bv_q, L0_attn_bv_k, L0_attn_bv_v,
           L0_ffn_route_w, L0_ffn_A, L0_ffn_Bw):
    inputs = dict(L0_attn_x=np.asarray(L0_attn_x, np.float32),
                  L0_attn_bv_q=np.asarray(L0_attn_bv_q, np.float32),
                  L0_attn_bv_k=np.asarray(L0_attn_bv_k, np.float32),
                  L0_attn_bv_v=np.asarray(L0_attn_bv_v, np.float32),
                  L0_ffn_route_w=np.asarray(L0_ffn_route_w, np.float32),
                  L0_ffn_A=np.asarray(L0_ffn_A, np.float32),
                  L0_ffn_Bw=np.asarray(L0_ffn_Bw, np.float32))
    in_maps = _stage(inputs)
    nc = build_program()
    res = run_bass_kernel_spmd(nc, in_maps, core_ids=list(range(8)))
    out = np.empty((2, 8192, D), np.float32)
    for c in range(8):
        o = res.results[c]["out_fm"]
        out[0, 1024 * c:1024 * (c + 1)] = o[:, 0:1024].T
        out[1, 1024 * (7 - c):1024 * (8 - c)] = o[:, 1024:2048].T
    return out
